# version 36
# baseline (speedup 1.0000x reference)
import numpy as np
import ml_dtypes
from contextlib import ExitStack

import concourse.bass as bass
import concourse.mybir as mybir
from concourse.bass_utils import run_bass_kernel_spmd

F32 = mybir.dt.float32
BF16 = mybir.dt.bfloat16
FP8 = mybir.dt.float8e4
U8 = mybir.dt.uint8
AF = mybir.ActivationFunctionType
ALU = mybir.AluOpType

S = 2048
D = 1024
H = 16
DH = 64
NT = S // 128
ND = D // 128
NP = H // 2
NC = S // 128
ENGINES = ("sync", "tensor", "scalar", "vector", "gpsimd")


class Plan:

    def __init__(self, nc, es):
        self.nc = nc
        self.ops = {e: [] for e in ENGINES}
        self.sems = {}
        self.count = {}
        self.hw = {e: {} for e in ENGINES}
        self.es = es

    def sem(self, name):
        if name not in self.sems:
            self.sems[name] = self.es.enter_context(self.nc.semaphore(name))
            self.count[name] = 0
        return self.sems[name]

    def add(self, engine, fn, waits=(), inc=None, inc_n=1, nop_inc=False):
        import os as _os
        if _os.environ.get("K_SEQ", "0") == "1":
            waits = list(waits) + [(sn, c) for sn, c in self.count.items() if c > 0]
        w = []
        for sname, val in waits:
            if val <= 0:
                continue
            if self.hw[engine].get(sname, 0) >= val:
                continue
            self.hw[engine][sname] = val
            w.append((self.sem(sname), val))
        pair = None
        if inc is not None:
            s = self.sem(inc)
            self.count[inc] += inc_n
            pair = (s, inc_n)
        self.ops[engine].append((w, fn, pair, nop_inc))

    def emit(self):
        nc = self.nc
        with nc.Block() as blk:
            for ename in ENGINES:
                olist = self.ops[ename]
                if not olist:
                    continue

                def make(olist, ename=ename):
                    import os as _os
                    nopform = _os.environ.get("K_NOPINC", "0") == "1"
                    def body(eng):
                        for w, fn, pair, nop_inc in olist:
                            for sem, val in w:
                                eng.wait_ge(sem, val)
                            inst = fn(eng)
                            if pair is not None:
                                if inst is None or nop_inc or (
                                        nopform and ename in ("tensor", "scalar", "vector", "gpsimd")):
                                    inst = eng.nop()
                                inst.then_inc(*pair)
                    return body

                getattr(blk, ename)(make(olist))
        self.ops = {e: [] for e in ENGINES}


def build_nc():
    nc = bass.Bass("TRN2", target_bir_lowering=False,
                   detect_race_conditions=False)

    x_d = nc.dram_tensor("x", [S, D], F32, kind="ExternalInput").ap()
    wqkv_d = nc.dram_tensor("wqkv", [D, 3 * D], F32, kind="ExternalInput").ap()
    wout_d = nc.dram_tensor("wout", [D, D], F32, kind="ExternalInput").ap()
    wffn_d = nc.dram_tensor("wffn", [D, D], U8, kind="ExternalInput").ap()
    bias_d = nc.dram_tensor("bias", [128, D], F32, kind="ExternalInput").ap()
    wsc_d = nc.dram_tensor("wscale", [128, 1], F32, kind="ExternalInput").ap()
    out_d = nc.dram_tensor("out", [S, D], F32, kind="ExternalOutput").ap()
    import os
    DBG_LVL = int(os.environ.get("K_DEBUG", "0"))
    DBG = DBG_LVL == 1
    DBG2 = DBG_LVL >= 1
    DBG_P = int(os.environ.get("K_DBG_P", "-1"))
    if DBG2:
        dbg_at2 = nc.dram_tensor("dbg_at2", [128, NP * S * 2], U8, kind="ExternalOutput").ap()
        dbg_r1 = nc.dram_tensor("dbg_r1", [S, D], F32, kind="ExternalOutput").ap()
        dbg_y1b = nc.dram_tensor("dbg_y1b", [S, D], F32, kind="ExternalOutput").ap()
        dbg_t2 = nc.dram_tensor("dbg_t2", [S, D], F32, kind="ExternalOutput").ap()
        dbg_stt = nc.dram_tensor("dbg_stt", [16, 128, 16], F32, kind="ExternalOutput").ap()
    if DBG:
        dbg_pt = nc.dram_tensor("dbg_pt", [4, 128, 768 * 2], U8, kind="ExternalOutput").ap()
        dbg_ps = nc.dram_tensor("dbg_ps", [2, 128, 512], F32, kind="ExternalOutput").ap()
        dbg_pv = nc.dram_tensor("dbg_pv", [65, 128], F32, kind="ExternalOutput").ap()
        dbg_qk = nc.dram_tensor("dbg_qk", [128, 4 * NP * S], U8, kind="ExternalOutput").ap()
        dbg_v = nc.dram_tensor("dbg_v", [128, H * 130], U8, kind="ExternalOutput").ap()
        dbg_sc = nc.dram_tensor("dbg_sc", [2 * 65, S], F32, kind="ExternalOutput").ap()
        dbg_at = nc.dram_tensor("dbg_at", [128, NP * S * 2], U8, kind="ExternalOutput").ap()
        dbg_y1 = nc.dram_tensor("dbg_y1", [128, D], F32, kind="ExternalOutput").ap()
        dbg_wt = nc.dram_tensor("dbg_wt", [128, ND * D * 2], U8, kind="ExternalOutput").ap()

    id32_d = nc.inline_tensor(np.eye(128, dtype=np.float32), name="id32c").ap()
    id16_d = nc.inline_tensor(
        np.eye(128, dtype=ml_dtypes.bfloat16).view(np.uint16), name="id16c").ap()
    id8_d = nc.inline_tensor(
        np.eye(128, dtype=ml_dtypes.float8_e4m3).view(np.uint8), name="id8c").ap()
    ii = np.arange(128)
    mdiag = (ii[None, :] >= ii[:, None]).astype(ml_dtypes.bfloat16)
    mup = (ii[None, :] < ii[:, None]).astype(ml_dtypes.bfloat16)
    maskD_d = nc.inline_tensor(
        np.concatenate([mdiag, mdiag], 1).view(np.uint16), name="maskDc").ap()
    maskU_d = nc.inline_tensor(
        np.concatenate([mup, mup], 1).view(np.uint16), name="maskUc").ap()
    o65 = np.zeros((128, 128), np.float32)
    o65[64, :] = 1.0
    ones65_d = nc.inline_tensor(o65, name="ones65c").ap()
    ones_d = nc.inline_tensor(np.ones((128, 128), np.float32), name="onesc").ap()

    es_all = ExitStack()
    nc._kernel_es = es_all

    def sbuf(name, shape, dt, stack=es_all, side=None):
        return stack.enter_context(nc.sbuf_tensor(name, shape, dt, side=side))

    def psum(name, shape, dt=F32, stack=es_all):
        return stack.enter_context(nc.psum_tensor(name, shape, dt))

    P = Plan(nc, es_all)
    _selfc = {"vector": 0, "scalar": 0}
    _selfsem = {}

    def fence(e, engine, inst):
        if engine not in _selfsem:
            _selfsem[engine] = es_all.enter_context(nc.semaphore(f"SELF_{engine}"))
        _selfc[engine] += 1
        inst.then_inc(_selfsem[engine], 1)
        e.wait_ge(_selfsem[engine], _selfc[engine])

    id32 = sbuf("id32", [128, 128], F32)
    id16 = sbuf("id16", [128, 128], BF16)
    id8 = sbuf("id8", [128, 128], FP8)
    maskD = sbuf("maskD", [128, 256], BF16)
    maskU = sbuf("maskU", [128, 256], BF16)
    ones65f = sbuf("ones65f", [128, 128], F32)
    ones32 = sbuf("ones32", [128, 128], F32)
    WT = sbuf("WT", [128, ND * D], BF16)
    bias_row = sbuf("bias_row", [128, D], F32)
    bias_full = sbuf("bias_full", [128, D], F32)
    ws_sc = sbuf("ws_sc", [128, 1], F32)
    ws_col = sbuf("ws_col", [128, 1], F32)

    esAB = ExitStack()
    xT = sbuf("xT", [128, ND * S], BF16, esAB)
    wqb = sbuf("wqb", [128, ND * 3 * D], BF16, esAB)
    esA = ExitStack()
    x_ring = [sbuf(f"xr{i}", [128, D], F32, esA) for i in range(4)]
    wq_stage = [sbuf(f"wqs{i}", [128, 3 * D], F32, esA) for i in range(3)]
    wu8 = [sbuf(f"wu8_{i}", [128, D], U8, esA) for i in range(4)]
    dec_f = sbuf("dec_f", [128, D], F32, esA)
    dec_sm = sbuf("dec_sm", [128, D], F32, esA)
    dec_a = sbuf("dec_a", [128, D], F32, esA)
    dec_tm = sbuf("dec_tm", [128, D], F32, esA)
    dec_b = sbuf("dec_b", [128, D], U8, esA)
    dec_vm = sbuf("dec_vm", [128, D], F32, esA)
    wbf = [sbuf(f"wbf{i}", [128, D], F32, esA) for i in range(2)]
    pT = [psum(f"pT{i}", [128, 512], F32, esA) for i in range(2)]
    pWT = [psum(f"pWT{i}", [128, 512], F32, esA) for i in range(2)]
    pB = psum("pB", [128, 1024], F32, esA)

    for dst, src in ((id32, id32_d), (maskD.bitcast(U8), maskD_d.bitcast(U8)),
                     (maskU.bitcast(U8), maskU_d.bitcast(U8)),
                     (ones65f, ones65_d), (ones32, ones_d),
                     (bias_row, bias_d), (ws_sc, wsc_d),
                     (id16.bitcast(U8), id16_d.bitcast(U8)),
                     (id8.bitcast(U8), id8_d)):
        P.add("sync", (lambda d, s: lambda e: e.dma_start(out=d[:], in_=s[:]))(dst, src),
              inc="CONST", inc_n=16)
    ld_consts = P.count["CONST"]

    P.add("tensor", lambda e: e.matmul(out=pB[0:128, 0:512], lhsT=ones32[0:1, :],
                                       rhs=bias_row[0:1, 0:512], start=True, stop=True),
          waits=[("CONST", ld_consts)], inc="MM")
    P.add("tensor", lambda e: e.matmul(out=pB[0:128, 512:1024], lhsT=ones32[0:1, :],
                                       rhs=bias_row[0:1, 512:1024], start=True, stop=True),
          inc="MM")
    mm_bias = P.count["MM"]
    P.add("scalar", lambda e: e.copy(out=bias_full[:], in_=pB[:]),
          waits=[("MM", mm_bias)], inc="BIASF")
    P.add("tensor", lambda e: e.matmul(out=pB[0:128, 0:1], lhsT=ones32[0:1, :],
                                       rhs=ws_sc[0:1, 0:1], start=True, stop=True),
          waits=[("BIASF", 1)], inc="MM")
    mm_ws = P.count["MM"]
    P.add("scalar", lambda e: e.copy(out=ws_col[:], in_=pB[0:128, 0:1]),
          waits=[("MM", mm_ws)], inc="BIASF")

    ld_x = {}
    xtp_after = {}
    xev_after = {}
    ld_wq = {}
    wqc_after = {}
    ld_w8 = {}
    wdc_after = {}
    wd1_after = {}
    wdec_after = {}
    wtp_after = {}
    wtev_after = {}

    def emit_xtile(tt):
        r = tt % 4
        waits = []
        if tt >= 4:
            waits.append(("XTP", xtp_after[tt - 4]))
        P.add("sync",
              (lambda tt, r: lambda e: e.dma_start(out=x_ring[r][:],
                                                   in_=x_d[128 * tt:128 * (tt + 1), :]))(tt, r),
              waits=waits, inc=f"XD{r}", inc_n=16)
        ld_x[tt] = (f"XD{r}", P.count[f"XD{r}"])

    def emit_xtp(tt):
        r = tt % 4
        for half in range(2):
            def tp(tt=tt, r=r, half=half):
                def f(e):
                    inst = None
                    for dd in range(4):
                        d = 4 * half + dd
                        inst = e.transpose(out=pT[half][0:128, 128 * dd:128 * (dd + 1)],
                                           in_=x_ring[r][:, 128 * d:128 * (d + 1)],
                                           identity=id32[:])
                    return inst
                return f
            waits = [ld_x[tt]]
            if tt >= 1:
                waits.append(("XEV", xev_after.get((tt - 1, half), 0)))
            P.add("tensor", tp(), waits=waits, inc="XTP")
            xtp_after[tt] = P.count["XTP"]
            def ev(tt=tt, half=half):
                def f(e):
                    dst = xT[:].rearrange("p (d t) -> p d t", t=S)[
                        :, 4 * half:4 * half + 4, 128 * tt:128 * (tt + 1)]
                    return e.copy(out=dst, in_=pT[half][:].rearrange(
                        "p (d t) -> p d t", t=128))
                return f
            P.add("scalar", ev(), waits=[("XTP", P.count["XTP"])], inc="XEV")
            xev_after[(tt, half)] = P.count["XEV"]

    def emit_wtp(wn):
        r = wn % 2
        for half in range(2):
            def wtp(wn=wn, r=r, half=half):
                def f(e):
                    inst = None
                    for dd in range(4):
                        d = 4 * half + dd
                        inst = e.transpose(out=pWT[half][0:128, 128 * dd:128 * (dd + 1)],
                                           in_=wbf[r][:, 128 * d:128 * (d + 1)],
                                           identity=id32[:])
                    return inst
                return f
            waits = [("WDEC", wdec_after[wn])]
            if wn >= 1:
                waits.append(("WTEV", wtev_after.get((wn - 1, half), 0)))
            P.add("tensor", wtp(), waits=waits, inc="WTP")

            def wev(wn=wn, half=half):
                def f(e):
                    dst = WT[:].rearrange("p (d n) -> p d n", n=D)[
                        :, 4 * half:4 * half + 4, 128 * wn:128 * (wn + 1)]
                    return e.copy(out=dst, in_=pWT[half][:].rearrange(
                        "p (d n) -> p d n", n=128))
                return f
            P.add("scalar", wev(), waits=[("WTP", P.count["WTP"])], inc="WTEV")
            wtev_after[(wn, half)] = P.count["WTEV"]
        wtp_after[wn] = P.count["WTP"]

    for d in range(ND):
        r = d % 3
        r8 = d % 4
        waits = []
        if d >= 3:
            waits.append(("WQC", wqc_after[d - 3]))
        P.add("sync",
              (lambda d, r: lambda e: e.dma_start(out=wq_stage[r][:],
                                                  in_=wqkv_d[128 * d:128 * (d + 1), :]))(d, r),
              waits=waits, inc=f"WQ{r}", inc_n=16)
        ld_wq[d] = (f"WQ{r}", P.count[f"WQ{r}"])
        emit_xtile(2 * d)
        emit_xtile(2 * d + 1)
        waits = []
        if d >= 4:
            waits.append(("WDC", wdc_after[d - 4]))
        P.add("sync",
              (lambda d, r8: lambda e: e.dma_start(out=wu8[r8][:],
                                                   in_=wffn_d[128 * d:128 * (d + 1), :]))(d, r8),
              waits=waits, inc=f"W8{r8}", inc_n=16)
        ld_w8[d] = (f"W8{r8}", P.count[f"W8{r8}"])

        def wqcast(d=d, r=r, lo=0, hi=0):
            def f(e):
                return e.tensor_copy(out=wqb[:, 3 * D * d + lo:3 * D * d + hi],
                                     in_=wq_stage[r][:, lo:hi])
            return f

        def wqcast_a(d=d, r=r, lo=0, hi=0):
            def f(e):
                return e.copy(out=wqb[:, 3 * D * d + lo:3 * D * d + hi],
                              in_=wq_stage[r][:, lo:hi])
            return f
        P.add("scalar", wqcast_a(d, r, 0, 2304), waits=[ld_wq[d]], inc="WQC")
        P.add("gpsimd", wqcast(d, r, 2304, 3072), waits=[ld_wq[d]], inc="WQC")
        wqc_after[d] = P.count["WQC"]

        emit_xtp(2 * d)
        emit_xtp(2 * d + 1)
        if d >= 1:
            emit_wtp(d - 1)

        waits = [ld_w8[d]]
        if d >= 1:
            waits.append(("WD1", wd1_after[d - 1]))
        P.add("gpsimd",
              (lambda d, r8: lambda e: e.tensor_copy(out=dec_f[:], in_=wu8[r8][:]))(d, r8),
              waits=waits, inc="WDC")
        wdc_after[d] = P.count["WDC"]

        def dec1(d=d):
            def f(e):
                e.tensor_single_scalar(out=dec_sm[:], in_=dec_f[:], scalar=128.0,
                                       op=ALU.is_ge)
                e.scalar_tensor_tensor(out=dec_a[:], in0=dec_sm[:], scalar=-128.0,
                                       in1=dec_f[:], op0=ALU.mult, op1=ALU.add)
                e.tensor_single_scalar(out=dec_tm[:], in_=dec_a[:], scalar=120.0,
                                       op=ALU.is_ge)
                return e.scalar_tensor_tensor(out=dec_b[:], in0=dec_tm[:], scalar=-8.0,
                                              in1=dec_f[:], op0=ALU.mult, op1=ALU.add)
            return f
        waits = [("WDC", wdc_after[d])]
        P.add("vector", dec1(), waits=waits, inc="WD1")
        wd1_after[d] = P.count["WD1"]

        waits = [("WD1", wd1_after[d])]
        if d >= 1:
            waits.append(("WDEC", wdec_after[d - 1]))
        P.add("scalar", lambda e: e.copy(out=dec_vm[:], in_=dec_b[:].bitcast(FP8)),
              waits=waits, inc="WD2")
        wd2 = P.count["WD2"]

        def dec2(d=d):
            def f(e):
                return e.scalar_tensor_tensor(out=wbf[d % 2][:], in0=dec_tm[:],
                                              scalar=1.0, in1=dec_vm[:],
                                              op0=ALU.add, op1=ALU.mult)
            return f
        waits = [("WD2", wd2)]
        if d >= 2:
            waits.append(("WTP", wtp_after[d - 2]))
        P.add("vector", dec2(), waits=waits, inc="WDEC")
        wdec_after[d] = P.count["WDEC"]

    emit_wtp(ND - 1)

    P.emit()
    nc.all_engine_barrier()

    esA.close()
    esBC = ExitStack()
    qkT = sbuf("qkT", [128, 2 * NP * S], BF16, esBC, side="right")
    vpad = [sbuf(f"vp{t}", [128, H * 65], BF16, esBC, side="right") for t in range(NT)]
    esB = ExitStack()
    pQK = [psum(f"pQK{i}", [128, 512], F32, esB) for i in range(2)]
    pV = [psum(f"pV{i}", [128, 1024], F32, esB) for i in range(2)]

    def qT_ap(p):
        return qkT[:, 2 * S * p: 2 * S * p + S]

    def kT_ap(p):
        return qkT[:, 2 * S * p + S: 2 * S * (p + 1)]

    for tt in range(NT):
        P.add("vector",
              (lambda tt: lambda e: e.memset(
                  vpad[tt][:].rearrange("p (h c) -> p h c", c=65)[:, :, 64:65], 1.0))(tt),
              inc="VONE")
    vone = P.count["VONE"]

    qkev_after = {}
    for j in range(16):
        for s4 in range(4):
            idx = 4 * j + s4
            bank = idx % 2

            def qkmm(j=j, s4=s4, bank=bank):
                def f(e):
                    inst = None
                    for d in range(ND):
                        inst = e.matmul(
                            out=pQK[bank][:],
                            lhsT=wqb[:, 3 * D * d + 128 * j: 3 * D * d + 128 * (j + 1)],
                            rhs=xT[:, S * d + 512 * s4: S * d + 512 * (s4 + 1)],
                            start=(d == 0), stop=(d == ND - 1))
                    return inst
                return f
            waits = []
            if idx >= 2:
                waits.append(("QKEV", qkev_after[idx - 2]))
            P.add("tensor", qkmm(), waits=waits, inc="QKMM")

            def qkev(j=j, s4=s4, bank=bank):
                def f(e):
                    p, isk = j % 8, j >= 8
                    dst = (kT_ap(p) if isk else qT_ap(p))[:, 512 * s4:512 * (s4 + 1)]
                    return e.copy(out=dst, in_=pQK[bank][:])
                return f
            P.add("scalar", qkev(), waits=[("QKMM", P.count["QKMM"])], inc="QKEV")
            qkev_after[idx] = P.count["QKEV"]

    vev_after = {}
    for tt in range(NT):
        bank = tt % 2

        def vmm(tt=tt, bank=bank):
            def f(e):
                inst = None
                for s2 in range(2):
                    for d in range(ND):
                        inst = e.matmul(
                            out=pV[bank][:, 512 * s2:512 * (s2 + 1)],
                            lhsT=xT[:, S * d + 128 * tt: S * d + 128 * (tt + 1)],
                            rhs=wqb[:, 3 * D * d + 2 * D + 512 * s2:
                                    3 * D * d + 2 * D + 512 * (s2 + 1)],
                            start=(d == 0), stop=(d == ND - 1))
                return inst
            return f
        waits = []
        if tt >= 2:
            waits.append(("VEV", vev_after[tt - 2]))
        P.add("tensor", vmm(), waits=waits, inc="VMM")

        def vev(tt=tt, bank=bank):
            def f(e):
                dst = vpad[tt][:].rearrange("p (h c) -> p h c", c=65)[:, :, 0:64]
                return e.tensor_copy(out=dst, in_=pV[bank][:].rearrange(
                    "p (h c) -> p h c", c=64))
            return f
        P.add("vector", vev(), waits=[("VMM", P.count["VMM"]), ("VONE", vone)],
              inc="VEV")
        vev_after[tt] = P.count["VEV"]

    if DBG:
        P.add("sync", lambda e: e.dma_start(out=dbg_qk[:],
                                            in_=qkT[:].bitcast(U8)),
              waits=[("QKEV", qkev_after[63]), ("VEV", vev_after[0])],
              inc="LD", inc_n=16)
        P.add("sync", lambda e: e.dma_start(out=dbg_v[:],
                                            in_=vpad[0][:].bitcast(U8)), inc="LD", inc_n=16)
    P.emit()
    nc.all_engine_barrier()

    esAB.close()
    esB.close()
    esC = ExitStack()
    attn = sbuf("attn", [128, NP * S], BF16)
    woutb = sbuf("woutb", [128, ND * D], BF16)
    PT = [sbuf(f"PT{i}", [128, 768], BF16, esC, side="right") for i in range(4)]
    sc65 = [sbuf(f"sc65_{h}", [65, S], F32, esC, side="right") for h in range(2)]
    wo_stage = [sbuf(f"wos{i}", [128, 512], F32, esC, side="right") for i in range(2)]
    scB = [sbuf(f"scB{i}", [128, 512], F32, esC, side="right") for i in range(2)]
    rbuf = sbuf("rbuf", [128, 512], F32, esC, side="right")
    rscr = [sbuf(f"rscr{i}", [128, 512], F32, esC) for i in range(2)]
    pSb = [psum(f"pSb{i}", [128, 1024], F32, esC) for i in range(2)]
    pS = [pSb[0][:, 0:512], pSb[0][:, 512:1024], pSb[1][:, 0:512], pSb[1][:, 512:1024]]
    pPV = [psum(f"pPV{i}", [128, 512], F32, esC) for i in range(4)]
    pR2 = pPV

    woc_after = {}
    for dh in range(2 * ND):
        d, hf = dh // 2, dh % 2
        r = dh % 2
        waits = []
        if dh >= 2:
            waits.append(("WOC", woc_after[dh - 2]))
        P.add("sync",
              (lambda d, hf, r: lambda e: e.dma_start(
                  out=wo_stage[r][:],
                  in_=wout_d[128 * d:128 * (d + 1), 512 * hf:512 * (hf + 1)]))(d, hf, r),
              waits=waits, inc=f"WO{r}", inc_n=16)
        ldd = (f"WO{r}", P.count[f"WO{r}"])
        P.add("gpsimd",
              (lambda d, hf, r: lambda e: e.tensor_copy(
                  out=woutb[:, D * d + 512 * hf:D * d + 512 * (hf + 1)],
                  in_=wo_stage[r][:]))(d, hf, r),
              waits=[ldd], inc="WOC")
        woc_after[dh] = P.count["WOC"]

    mms_all = {}
    exp_after = {}
    msk_after = {}
    pvq_after = {}
    pve_after = {}
    nrm_after = {}
    r2_after = {}

    pending_nrm = []
    nrm_seq = [0]
    for p in range(NP):
        def emit_mmS(p, c):
            n_c = 128 * min(3, NC - c)
            bankA, bankB = pS[2 * (c % 2)], pS[2 * (c % 2) + 1]

            def f(e):
                e.matmul(out=bankA[0:128, 0:n_c],
                         lhsT=kT_ap(p)[0:64, 128 * c:128 * (c + 1)],
                         rhs=qT_ap(p)[0:64, 128 * c:128 * c + n_c],
                         start=True, stop=True, tile_position=(0, 0))
                return e.matmul(out=bankB[0:128, 0:n_c],
                                lhsT=kT_ap(p)[64:128, 128 * c:128 * (c + 1)],
                                rhs=qT_ap(p)[64:128, 128 * c:128 * c + n_c],
                                start=True, stop=True, tile_position=(64, 0))
            waits = []
            pp, cc = (p, c - 2) if c >= 2 else (p - 1, c + 14)
            if pp >= 0:
                waits.append(("EXP", 16 * pp + cc + 1))
            P.add("tensor", f, waits=waits, inc="MMS")
            return P.count["MMS"]

        def emit_exp(p, c, mms_count):
            n_c = 128 * min(3, NC - c)
            slot = PT[c % 4]
            bankA, bankB = pS[2 * (c % 2)], pS[2 * (c % 2) + 1]
            waits = [("MMS", mms_count)]
            pt_reader = (p, c - 2) if c >= 4 else ((p - 1, min(c + 14, 15)) if p > 0 else None)
            if pt_reader in pvq_after:
                waits.append(("PVQ", pvq_after[pt_reader]))
            if DBG and p == DBG_P and c < 4:
                waits.append(("DBGD", 32))
            if DBG and ((p == DBG_P and c >= 4) or (p == DBG_P + 1)):
                waits.append(("DBGD", 32 + 16 * min(4, c if p == DBG_P else c + 16)))

            def f(e):
                src = pSb[c % 2][:].rearrange("p (b c2) -> p b c2", c2=512)[:, :, 0:n_c]
                dst = slot[:].rearrange("p (b c2) -> p b c2", c2=384)[:, :, 0:n_c]
                return e.activation(out=dst, in_=src, func=AF.Exp, scale=0.125)
            P.add("scalar", f, waits=waits, inc="EXP")
            exp_after[(p, c)] = P.count["EXP"]

        def emit_mask(p, c):
            slot = PT[c % 4]

            def f(e):
                pv = slot[:].rearrange("p (a c) -> p a c", c=128)
                md = maskD[:].rearrange("p (a c) -> p a c", c=128)
                inst = e.tensor_mul(out=pv[:, 0::3, :], in0=pv[:, 0::3, :], in1=md)
                if c <= 13:
                    mu = maskU[:].rearrange("p (a c) -> p a c", c=128)
                    inst = e.tensor_mul(out=pv[:, 2::3, :], in0=pv[:, 2::3, :], in1=mu)
                return inst
            P.add("vector", f, waits=[("EXP", exp_after[(p, c)])], inc="MSK")
            msk_after[(p, c)] = P.count["MSK"]
            if DBG and p == DBG_P and c < 4:
                P.add("sync",
                      (lambda c: lambda e: e.dma_start(
                          out=dbg_pt[c], in_=PT[c][:].bitcast(U8)))(c),
                      waits=[("MSK", msk_after[(p, c)])], inc="DBGD", inc_n=16)

        def emit_pv(p, t):
            g = t // 4
            waits = [("MSK", msk_after[(p, t)])]
            pg = (p, g - 2) if g >= 2 else ((p - 1, g + 2) if p > 0 else None)
            if pg is not None:
                for h in range(2):
                    if (pg[0], h, pg[1]) in pve_after:
                        waits.append(("PVE", pve_after[(pg[0], h, pg[1])]))
            if p > 0 and g <= 1:
                for gp in (2 * 0 + g % 2, 2 * 1 + g % 2):
                    if (p - 1, gp) in nrm_after:
                        waits.append(("NRM", nrm_after[(p - 1, gp)]))

            def f(e):
                inst = None
                for h in range(2):
                    bank = pPV[2 * h + g % 2]
                    ccs = range(max(0, t - 2), t + 1)
                    for i, cc in enumerate(ccs):
                        inst = e.matmul(
                            out=bank[0:65, 128 * (t % 4):128 * (t % 4 + 1)],
                            lhsT=vpad[cc][:, 65 * (2 * p + h):65 * (2 * p + h + 1)],
                            rhs=PT[cc % 4][0:128, 384 * h + 128 * (t - cc):
                                           384 * h + 128 * (t - cc) + 128],
                            start=(i == 0), stop=(cc == t))
                return inst
            P.add("tensor", f, waits=waits, inc="PVQ")
            pvq_after[(p, t)] = P.count["PVQ"]

        def emit_pve(p, h, g):
            waits = [("PVQ", pvq_after[(p, 4 * g + 3)])]
            if p > 0:
                waits.append(("NRM", nrm_after[(p - 1, g)]))

            def f(e, h=h, g=g):
                return e.tensor_copy(out=sc65[h][:, 512 * g:512 * (g + 1)],
                                     in_=pPV[2 * h + g % 2][0:65, 0:512])
            P.add("vector", f, waits=waits, inc="PVE")
            pve_after[(p, h, g)] = P.count["PVE"]

        def emit_normblock(p, g, qc):
            nb = nrm_seq[0]
            nrm_seq[0] += 1
            sbw = [("PVE", pve_after[(p, 1, g)])]
            if DBG and p == DBG_P and g == 0:
                sbw.append(("DBGD", 48 + 16))
            prev_cnt = {0: 4 * p - 2, 1: 4 * p, 2: 4 * p + 1, 3: 4 * p + 3}[g]
            if prev_cnt > 0:
                sbw.append(("NRM", prev_cnt))
            P.add("sync",
                  (lambda p, g, scB=scB, sc65=sc65: lambda e: e.dma_start(
                      out=scB[g % 2][64:128, :],
                      in_=sc65[1][0:64, 512 * g:512 * (g + 1)]))(p, g),
                  waits=sbw, inc=f"SB{g % 2}", inc_n=16)
            sbc = (f"SB{g % 2}", P.count[f"SB{g % 2}"])
            bankfree = pve_after[(p, g // 2, (g % 2) + 2)]
            zready = pve_after[(p, 1, g)]

            def bc(p=p, g=g, sc65=sc65):
                def f(e):
                    e.matmul(out=pR2[g][0:64, 0:512], lhsT=ones65f[64:65, 0:64],
                             rhs=sc65[0][64:65, 512 * g:512 * (g + 1)],
                             start=True, stop=True, tile_position=(64, 0))
                    return e.matmul(out=pR2[g][64:128, 0:512], lhsT=ones65f[64:65, 0:64],
                                    rhs=sc65[1][64:65, 512 * g:512 * (g + 1)],
                                    start=True, stop=True, tile_position=(64, 64))
                return f
            P.add("tensor", bc(), waits=[("PVE", max(bankfree, zready))], inc="R2")
            r2c = P.count["R2"]

            def rcp(g=g, nb=nb):
                def f(e):
                    e.activation(out=rscr[nb % 2][:], in_=pR2[g][:], func=AF.Ln)
                    return e.activation(out=rscr[nb % 2][:], in_=rscr[nb % 2][:],
                                        func=AF.Exp, scale=-1.0)
                return f
            rw = [("R2", r2c)]
            if nb >= 1:
                rw.append(("NRM", nb - 1))
            P.add("scalar", rcp(), waits=rw, inc="RB")
            rbc = P.count["RB"]

            def emit_nrm(p=p, g=g, nb=nb, rbc=rbc, sbc=sbc):
                def f(e):
                    e.tensor_mul(out=attn[0:64, S * p + 512 * g:S * p + 512 * (g + 1)],
                                 in0=sc65[0][0:64, 512 * g:512 * (g + 1)],
                                 in1=rscr[nb % 2][0:64, 0:512])
                    return e.tensor_mul(
                        out=attn[64:128, S * p + 512 * g:S * p + 512 * (g + 1)],
                        in0=scB[g % 2][64:128, 0:512],
                        in1=rscr[nb % 2][64:128, 0:512])
                P.add("vector", f, waits=[("RB", rbc), sbc], inc="NRM")
                nrm_after[(p, g)] = P.count["NRM"]
            pending_nrm.append((qc, emit_nrm))

        mms_c = {}
        for c in range(3):
            mms_c[c] = emit_mmS(p, c)
        if DBG and p == DBG_P:
            def snap_ps(e):
                e.copy(out=wo_stage[0][:, 0:384], in_=pS[0][0:128, 0:384])
                return e.copy(out=wo_stage[1][:, 0:384], in_=pS[1][0:128, 0:384])
            P.add("scalar", snap_ps, waits=[("MMS", mms_c[0]), ("WOC", 8)], inc="DBGPS")
            P.add("sync", lambda e: e.dma_start(out=dbg_ps[0], in_=wo_stage[0][:, 0:512]),
                  waits=[("DBGPS", 1)], inc="DBGD", inc_n=16)
            P.add("sync", lambda e: e.dma_start(out=dbg_ps[1], in_=wo_stage[1][:, 0:512]),
                  inc="DBGD", inc_n=16)
        for c in range(NC):
            gc = NC * p + c
            while pending_nrm and gc - pending_nrm[0][0] >= 2:
                pending_nrm.pop(0)[1]()
            emit_exp(p, c, mms_c[c])
            emit_mask(p, c)
            emit_pv(p, c)
            if (c + 1) % 4 == 0:
                g = c // 4
                for h in range(2):
                    emit_pve(p, h, g)
                if g == 2:
                    emit_normblock(p, 0, gc)
                    emit_normblock(p, 2, gc)
                elif g == 3:
                    emit_normblock(p, 1, gc)
                    emit_normblock(p, 3, gc)
            if c + 3 < NC:
                mms_c[c + 3] = emit_mmS(p, c + 3)

        if DBG and p == DBG_P:
            def snap_pv(e):
                return e.copy(out=scB[0][0:65, 0:128], in_=pPV[0][0:65, 0:128])
            P.add("scalar", snap_pv, waits=[("PVQ", pvq_after[(p, 0)])], inc="DBGPV")
            P.add("sync", lambda e: e.dma_start(out=dbg_pv[:], in_=scB[0][0:65, 0:128]),
                  waits=[("DBGPV", 1)], inc="DBGD", inc_n=16)
        r2_after[p] = P.count["R2"]

    while pending_nrm:
        pending_nrm.pop(0)[1]()

    if DBG2:
        P.add("sync", lambda e: e.dma_start(out=dbg_at2[:], in_=attn[:].bitcast(U8)),
              waits=[("NRM", 32)], inc="LD", inc_n=16)
    if DBG:
        dbg_qk2 = nc.dram_tensor("dbg_qk2", [128, 4 * NP * S], U8, kind="ExternalOutput").ap()
        P.add("sync", lambda e: e.dma_start(out=dbg_qk2[:], in_=qkT[:].bitcast(U8)),
              waits=[("NRM", P.count["NRM"])], inc="LD", inc_n=16)
        P.add("sync", lambda e: e.dma_start(out=dbg_sc[0:65, :], in_=sc65[0][:]),
              waits=[("NRM", 4)], inc="LD", inc_n=16)
        P.add("sync", lambda e: e.dma_start(out=dbg_sc[65:130, :], in_=sc65[1][:]),
              inc="LD", inc_n=16)
        P.add("sync", lambda e: e.dma_start(out=dbg_at[:], in_=attn[:].bitcast(U8)),
              waits=[("NRM", P.count["NRM"])], inc="LD", inc_n=16)
        P.add("sync", lambda e: e.dma_start(out=dbg_wt[:], in_=WT[:].bitcast(U8)),
              inc="LD", inc_n=16)
    P.emit()
    nc.all_engine_barrier()

    esC.close()
    esBC.close()
    esD = ExitStack()
    x_res = [sbuf(f"xres{i}", [128, D], F32, esD, side="right") for i in range(4)]
    y1 = [sbuf(f"y1_{i}", [128, D], F32, esD, side="right") for i in range(3)]
    r1buf = [sbuf(f"r1_{i}", [128, D], F32, esD, side="right") for i in range(2)]
    r2buf = [sbuf(f"r2_{i}", [128, D], F32, esD, side="right") for i in range(2)]
    sq_scr = sbuf("sq_scr", [128, D], F32, esD, side="right")
    qin = [sbuf(f"qin{i}", [128, D], U8, esD, side="right") for i in range(2)]
    qinT = [sbuf(f"qinT{i}", [128, D], BF16, esD, side="right") for i in range(2)]
    outb = [sbuf(f"outb{i}", [128, D], F32, esD, side="right") for i in range(4)]
    stat = sbuf("stat", [128, 16], F32, esD, side="right")
    eps_c = sbuf("eps_c", [128, 1], F32, esD, side="right")
    pPR = [psum("pPR0", [128, 1024], F32, esD)] * 2
    pFF_ = [psum(f"pFF{i}", [128, 1024], F32, esD) for i in range(2)]
    pQ8 = [psum(f"pQ8{i}", [128, 1024], FP8, esD) for i in range(2)]

    P.add("vector", lambda e: e.memset(eps_c[:], 1e-5), inc="EPS")

    DF = float(D)
    EPS = 1e-5
    ST = {"s1": 0, "sq1": 1, "negm1": 2, "eb1": 3, "rstd1": 4, "xswsC": 5,
          "amax": 6, "chalf": 7, "xsws": 8, "s2": 9, "sq2": 10, "negm2": 11,
          "eb2": 12, "rstd2": 13, "xswsB": 15}
    XSW = ["xsws", "xswsB", "xswsC"]
    bnbuf = sbuf("bnbuf", [128, 12], F32, esD, side="right")

    def stc(name):
        return stat[:, ST[name]:ST[name] + 1]

    ldx_after = {}
    sq1_after = {}
    sq2_after = {}
    prmm_after = {}
    r1_after = {}
    ln1s_after = {}
    ln1r_after = {}
    y1_after = {}
    q8_after = {}
    qtp_after = {}
    qte_after = {}
    ffmm_after = {}
    ff1_after = {}
    r2s_after = {}
    ln2s_after = {}
    ln2r_after = {}
    as2_after = {}
    outv_after = {}
    out_after = {}
    st_count = 0

    def emit_xr(tt):
        xslot = tt % 4
        waits = []
        if tt >= 4:
            waits.append(("R1", r1_after.get(tt - 4, 0)))
        P.add("sync",
              (lambda tt, xslot: lambda e: e.dma_start(
                  out=x_res[xslot][:], in_=x_d[128 * tt:128 * (tt + 1), :]))(tt, xslot),
              waits=waits, inc=f"XR{xslot}", inc_n=16)
        ldx_after[tt] = (f"XR{xslot}", P.count[f"XR{xslot}"])

    def emit_preA(tt):
        bank = tt % 2
        def prmm(tt=tt, bank=bank):
            def f(e):
                inst = None
                for s2 in range(2):
                    for d in range(ND):
                        inst = e.matmul(
                            out=pPR[bank][:, 512 * s2:512 * (s2 + 1)],
                            lhsT=attn[:, S * d + 128 * tt:S * d + 128 * (tt + 1)],
                            rhs=woutb[:, D * d + 512 * s2:D * d + 512 * (s2 + 1)],
                            start=(d == 0), stop=(d == ND - 1))
                return inst
            return f
        waits = [("WOC", 16)]
        if tt >= 1:
            waits.append(("R1", r1_after[tt - 1]))
        P.add("tensor", prmm(), waits=waits, inc="PRMM")
        prmm_after[tt] = P.count["PRMM"]

        def r1op(tt=tt, bank=bank):
            def f(e):
                return e.scalar_tensor_tensor(
                    out=r1buf[bank][:], in0=pPR[bank][:], scalar=1.0,
                    in1=x_res[tt % 4][:], op0=ALU.mult, op1=ALU.add)
            return f
        r1w = [("PRMM", prmm_after[tt]), ldx_after[tt], ("EPS", 1)]
        if tt >= 2:
            r1w.append(("Y1", y1_after.get(tt - 2, 0)))
        P.add("vector", r1op(), waits=r1w, inc="R1")
        r1_after[tt] = P.count["R1"]

        def ln1stats(bank=bank):
            def f(e):
                e.bn_stats(out=bnbuf[:, 0:6], in_=r1buf[bank][:, 0:512])
                fence(e, "vector", e.bn_stats(out=bnbuf[:, 6:12], in_=r1buf[bank][:, 512:1024]))
                fence(e, "vector", e.bn_aggr(out=stat[:, 0:2], in_=bnbuf[:]))
                return e.tensor_scalar(out=stc("negm1"), in0=stc("s1"), scalar1=-1.0,
                                       scalar2=None, op0=ALU.mult)
            return f
        P.add("vector", ln1stats(), waits=[("R1", r1_after[tt])], inc="LN1S")
        ln1s_after[tt] = P.count["LN1S"]

        def ln1rstd(e):
            e.activation(out=stc("rstd1"), in_=stc("sq1"), func=AF.Ln, bias=eps_c[:])
            return e.activation(out=stc("rstd1"), in_=stc("rstd1"), func=AF.Exp, scale=-0.5)
        lrw = [("LN1S", ln1s_after[tt])]
        if tt >= 1:
            lrw.append(("Y1", y1_after.get(tt - 1, 0)))
        P.add("scalar", ln1rstd, waits=lrw, inc="LN1R")
        ln1r_after[tt] = P.count["LN1R"]

        def y1op(tt=tt, bank=bank):
            def f(e):
                return e.tensor_scalar(out=y1[tt % 3][:], in0=r1buf[bank][:],
                                       scalar1=stc("negm1"), scalar2=stc("rstd1"),
                                       op0=ALU.add, op1=ALU.mult)
            return f
        waits = [("LN1R", ln1r_after[tt])]
        if tt >= 3:
            waits.append(("FF1", ff1_after.get(tt - 3, 0)))
        P.add("vector", y1op(), waits=waits, inc="Y1")
        y1_after[tt] = P.count["Y1"]

        if DBG2:
            P.add("sync", (lambda tt, bank: lambda e: e.dma_start(
                out=dbg_r1[128 * tt:128 * (tt + 1), :], in_=r1buf[bank][:]))(tt, bank),
                  waits=[("Y1", y1_after[tt])], inc="LD", inc_n=16)
        def quant(tt=tt):
            def f(e):
                fence(e, "vector", e.tensor_reduce(out=stc("amax"), in_=y1[tt % 3][:],
                                                   axis=mybir.AxisListType.X, op=ALU.max,
                                                   apply_absolute_value=True))
                fence(e, "vector", e.reciprocal(out=stc("chalf"), in_=stc("amax")))
                e.tensor_scalar(out=stc("chalf"), in0=stc("chalf"), scalar1=224.0,
                                scalar2=None, op0=ALU.mult)
                return e.scalar_tensor_tensor(
                    out=stc(XSW[tt % 3]), in0=stc("amax"),
                    scalar=1.0 / 448.0, in1=ws_col[:], op0=ALU.mult, op1=ALU.mult)
            return f
        waits = [("Y1", y1_after[tt]), ("BIASF", 2)]
        P.add("vector", quant(), waits=waits, inc="CHF")
        chf = P.count["CHF"]

        def qmul(tt=tt):
            def f(e):
                return e.mul(out=qin[tt % 2][:].bitcast(FP8), in_=y1[tt % 3][:],
                             mul=stc("chalf"))
            return f
        qw = [("CHF", chf)]
        if tt >= 2:
            qw.append(("QTP", qtp_after.get(tt - 2, 0)))
        P.add("scalar", qmul(), waits=qw, inc="Q8")
        q8_after[tt] = P.count["Q8"]

    def emit_preB(tt):
        bank = tt % 2
        for half in range(2):
            def qtp(tt=tt, half=half):
                def f(e):
                    inst = None
                    for dd in range(4):
                        d = 4 * half + dd
                        inst = e.transpose(
                            out=pQ8[half][0:128, 256 * dd:256 * (dd + 1):2],
                            in_=qin[tt % 2][:, 128 * d:128 * (d + 1)].bitcast(FP8),
                            identity=id8[:])
                    return inst
                return f
            waits = [("Q8", q8_after[tt])]
            if tt >= 1:
                waits.append(("QTE", qte_after.get((tt - 1, half), 0)))
            P.add("tensor", qtp(), waits=waits, inc="QTP")
            qtp_after[tt] = P.count["QTP"]

            def qte(tt=tt, half=half):
                def f(e):
                    dst = qinT[tt % 2][:].rearrange("p (d t) -> p d t", t=128)[
                        :, 4 * half:4 * half + 4, :]
                    src = pQ8[half][:].rearrange("p (d t) -> p d t", t=256)[:, :, 0:256:2]
                    return e.activation(out=dst, in_=src, func=AF.Copy, scale=2.0)
                return f
            qtew = [("QTP", P.count["QTP"])]
            if tt >= 2:
                qtew.append(("FFMM", ffmm_after.get(tt - 2, 0)))
            P.add("scalar", qte(), waits=qtew, inc="QTE")
            qte_after[(tt, half)] = P.count["QTE"]

        if DBG2:
            P.add("sync", (lambda tt: lambda e: e.dma_start(
                out=dbg_y1b[128 * tt:128 * (tt + 1), :], in_=y1[tt % 3][:]))(tt),
                  waits=[("Q8", q8_after[tt])], inc="LD", inc_n=16)
            P.add("sync", (lambda tt: lambda e: e.dma_start(
                out=dbg_stt[tt], in_=stat[:]))(tt),
                  waits=[("Q8", q8_after[tt])], inc="LD", inc_n=16)
    def emit_post1(tt):
        def ffmm(tt=tt):
            def f(e):
                inst = None
                for s2 in range(2):
                    for d in range(ND):
                        inst = e.matmul(
                            out=pFF_[tt % 2][:, 512 * s2:512 * (s2 + 1)],
                            lhsT=qinT[tt % 2][:, 128 * d:128 * (d + 1)],
                            rhs=WT[:, D * d + 512 * s2:D * d + 512 * (s2 + 1)],
                            start=(d == 0), stop=(d == ND - 1))
                return inst
            return f
        waits = [("QTE", qte_after[(tt, 1)])]
        if tt >= 2:
            waits.append(("FF1", ff1_after.get(tt - 2, 0)))
        P.add("tensor", ffmm(), waits=waits, inc="FFMM")
        ffmm_after[tt] = P.count["FFMM"]

        def ff1(tt=tt):
            def f(e):
                return e.scalar_tensor_tensor(
                    out=r2buf[tt % 2][:], in0=pFF_[tt % 2][:],
                    scalar=stc(XSW[tt % 3]),
                    in1=y1[tt % 3][:], op0=ALU.mult, op1=ALU.add)
            return f
        ffw = [("FFMM", ffmm_after[tt])]
        if tt >= 2:
            ffw.append(("AS2", as2_after.get(tt - 2, 0)))
        P.add("vector", ff1(), waits=ffw, inc="FF1")
        ff1_after[tt] = P.count["FF1"]

    def emit_post2(tt):
        bank = tt % 2
        def r2op(tt=tt):
            def f(e):
                return e.tensor_add(out=r2buf[tt % 2][:], in0=r2buf[tt % 2][:],
                                    in1=bias_full[:])
            return f
        P.add("gpsimd", r2op(), waits=[("FF1", ff1_after[tt]), ("BIASF", 1)], inc="R2S")
        r2s_after[tt] = P.count["R2S"]

        if DBG2:
            P.add("sync", (lambda tt: lambda e: e.dma_start(
                out=dbg_t2[128 * tt:128 * (tt + 1), :], in_=r2buf[tt % 2][:]))(tt),
                  waits=[("R2S", P.count["R2S"])], inc="LD", inc_n=16)

        def as2(tt=tt):
            def f(e):
                e.activation(out=sq_scr[:], in_=r2buf[tt % 2][:], func=AF.Square,
                             accum_out=stc("sq2"))
                return e.activation(out=sq_scr[:], in_=r2buf[tt % 2][:], func=AF.Copy,
                                    accum_out=stc("s2"))
            return f
        asw = [("R2S", r2s_after[tt])]
        if tt >= 1:
            asw.append(("LN2S", ln2s_after.get(tt - 1, 0)))
        P.add("scalar", as2(), waits=asw, inc="AS2")
        as2c = P.count["AS2"]
        as2_after[tt] = as2c

        if DBG and tt == 0:
            P.add("sync", (lambda tt: lambda e: e.dma_start(
                out=dbg_y1[:], in_=y1[tt % 3][:]))(tt),
                  waits=[("Y1", y1_after[tt])], inc="LD", inc_n=16)

    def emit_post3(tt):
        def ln2smalls(e):
            e.tensor_mul(out=stc("eb2"), in0=stc("s2"), in1=stc("s2"))
            e.tensor_scalar(out=stc("eb2"), in0=stc("eb2"),
                            scalar1=-1.0 / (DF * DF), scalar2=EPS,
                            op0=ALU.mult, op1=ALU.add)
            return e.tensor_scalar(out=stc("negm2"), in0=stc("s2"),
                                   scalar1=-1.0 / DF, scalar2=None, op0=ALU.mult)
        P.add("vector", ln2smalls, waits=[("AS2", as2_after[tt])], inc="LN2S")
        ln2s_after[tt] = P.count["LN2S"]

        def ln2rstd(e):
            e.activation(out=stc("rstd2"), in_=stc("sq2"), func=AF.Ln,
                         scale=1.0 / DF, bias=stc("eb2"))
            return e.activation(out=stc("rstd2"), in_=stc("rstd2"), func=AF.Exp, scale=-0.5)
        l2w = [("LN2S", ln2s_after[tt])]
        if tt >= 1:
            l2w.append(("OUTV", outv_after.get(tt - 1, 0)))
        P.add("scalar", ln2rstd, waits=l2w, inc="LN2R")
        ln2r_after[tt] = P.count["LN2R"]

        def outop(tt=tt):
            def f(e):
                return e.tensor_scalar(out=outb[tt % 4][:], in0=r2buf[tt % 2][:],
                                       scalar1=stc("negm2"), scalar2=stc("rstd2"),
                                       op0=ALU.add, op1=ALU.mult)
            return f
        waits = [("LN2R", ln2r_after[tt])]
        if tt >= 4:
            waits.append(out_after[tt - 4])
        P.add("vector", outop(), waits=waits, inc="OUTV")
        outv = P.count["OUTV"]
        outv_after[tt] = outv
        P.add("sync",
              (lambda tt: lambda e: e.dma_start(
                  out=out_d[128 * tt:128 * (tt + 1), :], in_=outb[tt % 4][:]))(tt),
              waits=[("OUTV", outv)], inc=f"OD{tt % 2}", inc_n=16)
        out_after[tt] = (f"OD{tt % 2}", P.count[f"OD{tt % 2}"])


    for tt in range(NT + 4):
        if tt < NT:
            emit_xr(tt)
        if tt >= 4:
            emit_post3(tt - 4)
        if 3 <= tt <= NT + 2:
            emit_post2(tt - 3)
        if tt < NT:
            emit_preA(tt)
        if 1 <= tt <= NT:
            emit_preB(tt - 1)
        if 2 <= tt <= NT + 1:
            emit_post1(tt - 2)

    P.add("vector", lambda e: e.tensor_copy(out=sq_scr[:].bitcast(mybir.dt.int32)[0:32, 0:1],
                                            in_=stat[:].bitcast(mybir.dt.int32)[0:32, 0:1]),
          waits=[("LN2S", ln2s_after[NT - 1])])

    final_waits = [(sn, c) for sn, c in P.count.items()
                   if sn.startswith(("XD", "WQ", "W8", "WO", "XR", "OD", "SB")) or
                   sn in ("CONST", "LD", "DBGD")]
    P.add("sync", lambda e: None, waits=final_waits)
    P.emit()
    return nc


_NC_CACHE = None


def kernel(L0_attn_x, L0_attn_w_qkv, L0_attn_w_out, L0_ffn_weight,
           L0_ffn_weight_scale, L0_ffn_bias):
    global _NC_CACHE
    if _NC_CACHE is None:
        _NC_CACHE = build_nc()
    nc = _NC_CACHE

    x = np.ascontiguousarray(np.asarray(L0_attn_x))
    wqkv = np.ascontiguousarray(np.asarray(L0_attn_w_qkv))
    wout = np.ascontiguousarray(np.asarray(L0_attn_w_out))
    wffn = np.ascontiguousarray(np.asarray(L0_ffn_weight)).view(np.uint8)
    bias = np.ascontiguousarray(
        np.broadcast_to(np.asarray(L0_ffn_bias).reshape(1, D), (128, D)))
    wsc = np.ascontiguousarray(
        np.broadcast_to(np.asarray(L0_ffn_weight_scale).reshape(1, 1), (128, 1)))

    in_maps = []
    for i in range(8):
        in_maps.append({
            "x": np.ascontiguousarray(x[i]),
            "wqkv": wqkv, "wout": wout, "wffn": wffn,
            "bias": bias, "wscale": wsc,
        })
    res = run_bass_kernel_spmd(nc, in_maps, core_ids=list(range(8)))
    return np.stack([r["out"] for r in res.results], axis=0)




# version 37 (final)
# speedup vs baseline: 1.0447x; 1.0447x over previous
import numpy as np
import ml_dtypes
from contextlib import ExitStack

import concourse.bass as bass
import concourse.mybir as mybir
from concourse.bass_utils import run_bass_kernel_spmd

F32 = mybir.dt.float32
BF16 = mybir.dt.bfloat16
FP8 = mybir.dt.float8e4
U8 = mybir.dt.uint8
AF = mybir.ActivationFunctionType
ALU = mybir.AluOpType

S = 2048
D = 1024
H = 16
DH = 64
NT = S // 128
ND = D // 128
NP = H // 2
NC = S // 128
ENGINES = ("sync", "tensor", "scalar", "vector", "gpsimd")


class Plan:

    def __init__(self, nc, es):
        self.nc = nc
        self.ops = {e: [] for e in ENGINES}
        self.sems = {}
        self.count = {}
        self.hw = {e: {} for e in ENGINES}
        self.es = es

    def sem(self, name):
        if name not in self.sems:
            self.sems[name] = self.es.enter_context(self.nc.semaphore(name))
            self.count[name] = 0
        return self.sems[name]

    def add(self, engine, fn, waits=(), inc=None, inc_n=1, nop_inc=False):
        import os as _os
        if _os.environ.get("K_SEQ", "0") == "1":
            waits = list(waits) + [(sn, c) for sn, c in self.count.items() if c > 0]
        w = []
        for sname, val in waits:
            if val <= 0:
                continue
            if self.hw[engine].get(sname, 0) >= val:
                continue
            self.hw[engine][sname] = val
            w.append((self.sem(sname), val))
        pair = None
        if inc is not None:
            s = self.sem(inc)
            self.count[inc] += inc_n
            pair = (s, inc_n)
        self.ops[engine].append((w, fn, pair, nop_inc))

    def emit(self):
        nc = self.nc
        with nc.Block() as blk:
            for ename in ENGINES:
                olist = self.ops[ename]
                if not olist:
                    continue

                def make(olist, ename=ename):
                    import os as _os
                    nopform = _os.environ.get("K_NOPINC", "0") == "1"
                    def body(eng):
                        for w, fn, pair, nop_inc in olist:
                            for sem, val in w:
                                eng.wait_ge(sem, val)
                            inst = fn(eng)
                            if pair is not None:
                                if inst is None or nop_inc or (
                                        nopform and ename in ("tensor", "scalar", "vector", "gpsimd")):
                                    inst = eng.nop()
                                inst.then_inc(*pair)
                    return body

                getattr(blk, ename)(make(olist))
        self.ops = {e: [] for e in ENGINES}


def build_nc():
    nc = bass.Bass("TRN2", target_bir_lowering=False,
                   detect_race_conditions=False)

    x_d = nc.dram_tensor("x", [S, D], F32, kind="ExternalInput").ap()
    wqkv_d = nc.dram_tensor("wqkv", [D, 3 * D], F32, kind="ExternalInput").ap()
    wout_d = nc.dram_tensor("wout", [D, D], F32, kind="ExternalInput").ap()
    wffn_d = nc.dram_tensor("wffn", [D, D], U8, kind="ExternalInput").ap()
    bias_d = nc.dram_tensor("bias", [128, D], F32, kind="ExternalInput").ap()
    wsc_d = nc.dram_tensor("wscale", [128, 1], F32, kind="ExternalInput").ap()
    out_d = nc.dram_tensor("out", [S, D], F32, kind="ExternalOutput").ap()
    import os
    DBG_LVL = int(os.environ.get("K_DEBUG", "0"))
    DBG = DBG_LVL == 1
    DBG2 = DBG_LVL >= 1
    DBG_P = int(os.environ.get("K_DBG_P", "-1"))
    if DBG2:
        dbg_at2 = nc.dram_tensor("dbg_at2", [128, NP * S * 2], U8, kind="ExternalOutput").ap()
        dbg_r1 = nc.dram_tensor("dbg_r1", [S, D], F32, kind="ExternalOutput").ap()
        dbg_y1b = nc.dram_tensor("dbg_y1b", [S, D], F32, kind="ExternalOutput").ap()
        dbg_t2 = nc.dram_tensor("dbg_t2", [S, D], F32, kind="ExternalOutput").ap()
        dbg_stt = nc.dram_tensor("dbg_stt", [16, 128, 16], F32, kind="ExternalOutput").ap()
    if DBG:
        dbg_pt = nc.dram_tensor("dbg_pt", [4, 128, 768 * 2], U8, kind="ExternalOutput").ap()
        dbg_ps = nc.dram_tensor("dbg_ps", [2, 128, 512], F32, kind="ExternalOutput").ap()
        dbg_pv = nc.dram_tensor("dbg_pv", [65, 128], F32, kind="ExternalOutput").ap()
        dbg_qk = nc.dram_tensor("dbg_qk", [128, 4 * NP * S], U8, kind="ExternalOutput").ap()
        dbg_v = nc.dram_tensor("dbg_v", [128, H * 130], U8, kind="ExternalOutput").ap()
        dbg_sc = nc.dram_tensor("dbg_sc", [2 * 65, S], F32, kind="ExternalOutput").ap()
        dbg_at = nc.dram_tensor("dbg_at", [128, NP * S * 2], U8, kind="ExternalOutput").ap()
        dbg_y1 = nc.dram_tensor("dbg_y1", [128, D], F32, kind="ExternalOutput").ap()
        dbg_wt = nc.dram_tensor("dbg_wt", [128, ND * D * 2], U8, kind="ExternalOutput").ap()

    id32_d = nc.inline_tensor(np.eye(128, dtype=np.float32), name="id32c").ap()
    id16_d = nc.inline_tensor(
        np.eye(128, dtype=ml_dtypes.bfloat16).view(np.uint16), name="id16c").ap()
    id8_d = nc.inline_tensor(
        np.eye(128, dtype=ml_dtypes.float8_e4m3).view(np.uint8), name="id8c").ap()
    ii = np.arange(128)
    mdiag = (ii[None, :] >= ii[:, None]).astype(ml_dtypes.bfloat16)
    mup = (ii[None, :] < ii[:, None]).astype(ml_dtypes.bfloat16)
    maskD_d = nc.inline_tensor(
        np.concatenate([mdiag, mdiag], 1).view(np.uint16), name="maskDc").ap()
    maskU_d = nc.inline_tensor(
        np.concatenate([mup, mup], 1).view(np.uint16), name="maskUc").ap()
    o65 = np.zeros((128, 128), np.float32)
    o65[64, :] = 1.0
    ones65_d = nc.inline_tensor(o65, name="ones65c").ap()
    ones_d = nc.inline_tensor(np.ones((128, 128), np.float32), name="onesc").ap()

    es_all = ExitStack()
    nc._kernel_es = es_all

    def sbuf(name, shape, dt, stack=es_all, side=None):
        return stack.enter_context(nc.sbuf_tensor(name, shape, dt, side=side))

    def psum(name, shape, dt=F32, stack=es_all):
        return stack.enter_context(nc.psum_tensor(name, shape, dt))

    P = Plan(nc, es_all)
    _selfc = {"vector": 0, "scalar": 0}
    _selfsem = {}

    def fence(e, engine, inst):
        if engine not in _selfsem:
            _selfsem[engine] = es_all.enter_context(nc.semaphore(f"SELF_{engine}"))
        _selfc[engine] += 1
        inst.then_inc(_selfsem[engine], 1)
        e.wait_ge(_selfsem[engine], _selfc[engine])

    id32 = sbuf("id32", [128, 128], F32)
    id16 = sbuf("id16", [128, 128], BF16)
    id8 = sbuf("id8", [128, 128], FP8)
    maskD = sbuf("maskD", [128, 256], BF16)
    maskU = sbuf("maskU", [128, 256], BF16)
    ones65f = sbuf("ones65f", [128, 128], F32)
    ones32 = sbuf("ones32", [128, 128], F32)
    WT = sbuf("WT", [128, ND * D], BF16)
    bias_row = sbuf("bias_row", [128, D], F32)
    bias_full = sbuf("bias_full", [128, D], F32)
    ws_sc = sbuf("ws_sc", [128, 1], F32)
    ws_col = sbuf("ws_col", [128, 1], F32)

    esAB = ExitStack()
    xT = sbuf("xT", [128, ND * S], BF16, esAB)
    wqb = sbuf("wqb", [128, ND * 3 * D], BF16, esAB)
    esA = ExitStack()
    x_ring = [sbuf(f"xr{i}", [128, D], F32, esA) for i in range(4)]
    wq_stage = [sbuf(f"wqs{i}", [128, 3 * D], F32, esA) for i in range(3)]
    wu8 = [sbuf(f"wu8_{i}", [128, D], U8, esA) for i in range(4)]
    dec_f = sbuf("dec_f", [128, D], F32, esA)
    dec_sm = sbuf("dec_sm", [128, D], F32, esA)
    dec_a = sbuf("dec_a", [128, D], F32, esA)
    dec_tm = sbuf("dec_tm", [128, D], F32, esA)
    dec_b = sbuf("dec_b", [128, D], U8, esA)
    dec_vm = sbuf("dec_vm", [128, D], F32, esA)
    wbf = [sbuf(f"wbf{i}", [128, D], F32, esA) for i in range(2)]
    pT = [psum(f"pT{i}", [128, 512], F32, esA) for i in range(2)]
    pWT = [psum(f"pWT{i}", [128, 512], F32, esA) for i in range(2)]
    pB = psum("pB", [128, 1024], F32, esA)

    for dst, src in ((id32, id32_d), (maskD.bitcast(U8), maskD_d.bitcast(U8)),
                     (maskU.bitcast(U8), maskU_d.bitcast(U8)),
                     (ones65f, ones65_d), (ones32, ones_d),
                     (bias_row, bias_d), (ws_sc, wsc_d),
                     (id16.bitcast(U8), id16_d.bitcast(U8)),
                     (id8.bitcast(U8), id8_d)):
        P.add("sync", (lambda d, s: lambda e: e.dma_start(out=d[:], in_=s[:]))(dst, src),
              inc="CONST", inc_n=16)
    ld_consts = P.count["CONST"]

    P.add("tensor", lambda e: e.matmul(out=pB[0:128, 0:512], lhsT=ones32[0:1, :],
                                       rhs=bias_row[0:1, 0:512], start=True, stop=True),
          waits=[("CONST", ld_consts)], inc="MM")
    P.add("tensor", lambda e: e.matmul(out=pB[0:128, 512:1024], lhsT=ones32[0:1, :],
                                       rhs=bias_row[0:1, 512:1024], start=True, stop=True),
          inc="MM")
    mm_bias = P.count["MM"]
    P.add("scalar", lambda e: e.copy(out=bias_full[:], in_=pB[:]),
          waits=[("MM", mm_bias)], inc="BIASF")
    P.add("tensor", lambda e: e.matmul(out=pB[0:128, 0:1], lhsT=ones32[0:1, :],
                                       rhs=ws_sc[0:1, 0:1], start=True, stop=True),
          waits=[("BIASF", 1)], inc="MM")
    mm_ws = P.count["MM"]
    P.add("scalar", lambda e: e.copy(out=ws_col[:], in_=pB[0:128, 0:1]),
          waits=[("MM", mm_ws)], inc="BIASF")

    ld_x = {}
    xtp_after = {}
    xev_after = {}
    ld_wq = {}
    wqc_after = {}
    ld_w8 = {}
    wdc_after = {}
    wd1_after = {}
    wdec_after = {}
    wtp_after = {}
    wtev_after = {}

    def emit_xtile(tt):
        r = tt % 4
        waits = []
        if tt >= 4:
            waits.append(("XTP", xtp_after[tt - 4]))
        P.add("sync",
              (lambda tt, r: lambda e: e.dma_start(out=x_ring[r][:],
                                                   in_=x_d[128 * tt:128 * (tt + 1), :]))(tt, r),
              waits=waits, inc=f"XD{r}", inc_n=16)
        ld_x[tt] = (f"XD{r}", P.count[f"XD{r}"])

    def emit_xtp(tt):
        r = tt % 4
        for half in range(2):
            def tp(tt=tt, r=r, half=half):
                def f(e):
                    inst = None
                    for dd in range(4):
                        d = 4 * half + dd
                        inst = e.transpose(out=pT[half][0:128, 128 * dd:128 * (dd + 1)],
                                           in_=x_ring[r][:, 128 * d:128 * (d + 1)],
                                           identity=id32[:])
                    return inst
                return f
            waits = [ld_x[tt]]
            if tt >= 1:
                waits.append(("XEV", xev_after.get((tt - 1, half), 0)))
            P.add("tensor", tp(), waits=waits, inc="XTP")
            xtp_after[tt] = P.count["XTP"]
            def ev(tt=tt, half=half):
                def f(e):
                    dst = xT[:].rearrange("p (d t) -> p d t", t=S)[
                        :, 4 * half:4 * half + 4, 128 * tt:128 * (tt + 1)]
                    return e.copy(out=dst, in_=pT[half][:].rearrange(
                        "p (d t) -> p d t", t=128))
                return f
            P.add("scalar", ev(), waits=[("XTP", P.count["XTP"])], inc="XEV")
            xev_after[(tt, half)] = P.count["XEV"]

    def emit_wtp(wn):
        r = wn % 2
        for half in range(2):
            def wtp(wn=wn, r=r, half=half):
                def f(e):
                    inst = None
                    for dd in range(4):
                        d = 4 * half + dd
                        inst = e.transpose(out=pWT[half][0:128, 128 * dd:128 * (dd + 1)],
                                           in_=wbf[r][:, 128 * d:128 * (d + 1)],
                                           identity=id32[:])
                    return inst
                return f
            waits = [("WDEC", wdec_after[wn])]
            if wn >= 1:
                waits.append(("WTEV", wtev_after.get((wn - 1, half), 0)))
            P.add("tensor", wtp(), waits=waits, inc="WTP")

            def wev(wn=wn, half=half):
                def f(e):
                    dst = WT[:].rearrange("p (d n) -> p d n", n=D)[
                        :, 4 * half:4 * half + 4, 128 * wn:128 * (wn + 1)]
                    return e.copy(out=dst, in_=pWT[half][:].rearrange(
                        "p (d n) -> p d n", n=128))
                return f
            P.add("scalar", wev(), waits=[("WTP", P.count["WTP"])], inc="WTEV")
            wtev_after[(wn, half)] = P.count["WTEV"]
        wtp_after[wn] = P.count["WTP"]

    for d in range(ND):
        r = d % 3
        r8 = d % 4
        waits = []
        if d >= 3:
            waits.append(("WQC", wqc_after[d - 3]))
        P.add("sync",
              (lambda d, r: lambda e: e.dma_start(out=wq_stage[r][:],
                                                  in_=wqkv_d[128 * d:128 * (d + 1), :]))(d, r),
              waits=waits, inc=f"WQ{r}", inc_n=16)
        ld_wq[d] = (f"WQ{r}", P.count[f"WQ{r}"])
        emit_xtile(2 * d)
        emit_xtile(2 * d + 1)
        waits = []
        if d >= 4:
            waits.append(("WDC", wdc_after[d - 4]))
        P.add("sync",
              (lambda d, r8: lambda e: e.dma_start(out=wu8[r8][:],
                                                   in_=wffn_d[128 * d:128 * (d + 1), :]))(d, r8),
              waits=waits, inc=f"W8{r8}", inc_n=16)
        ld_w8[d] = (f"W8{r8}", P.count[f"W8{r8}"])

        def wqcast(d=d, r=r, lo=0, hi=0):
            def f(e):
                return e.tensor_copy(out=wqb[:, 3 * D * d + lo:3 * D * d + hi],
                                     in_=wq_stage[r][:, lo:hi])
            return f

        def wqcast_a(d=d, r=r, lo=0, hi=0):
            def f(e):
                return e.copy(out=wqb[:, 3 * D * d + lo:3 * D * d + hi],
                              in_=wq_stage[r][:, lo:hi])
            return f
        P.add("scalar", wqcast_a(d, r, 0, 2304), waits=[ld_wq[d]], inc="WQC")
        P.add("gpsimd", wqcast(d, r, 2304, 3072), waits=[ld_wq[d]], inc="WQC")
        wqc_after[d] = P.count["WQC"]

        emit_xtp(2 * d)
        emit_xtp(2 * d + 1)
        if d >= 1:
            emit_wtp(d - 1)

        waits = [ld_w8[d]]
        if d >= 1:
            waits.append(("WD1", wd1_after[d - 1]))
        P.add("gpsimd",
              (lambda d, r8: lambda e: e.tensor_copy(out=dec_f[:], in_=wu8[r8][:]))(d, r8),
              waits=waits, inc="WDC")
        wdc_after[d] = P.count["WDC"]

        def dec1(d=d):
            def f(e):
                e.tensor_single_scalar(out=dec_sm[:], in_=dec_f[:], scalar=128.0,
                                       op=ALU.is_ge)
                e.scalar_tensor_tensor(out=dec_a[:], in0=dec_sm[:], scalar=-128.0,
                                       in1=dec_f[:], op0=ALU.mult, op1=ALU.add)
                e.tensor_single_scalar(out=dec_tm[:], in_=dec_a[:], scalar=120.0,
                                       op=ALU.is_ge)
                return e.scalar_tensor_tensor(out=dec_b[:], in0=dec_tm[:], scalar=-8.0,
                                              in1=dec_f[:], op0=ALU.mult, op1=ALU.add)
            return f
        waits = [("WDC", wdc_after[d])]
        P.add("vector", dec1(), waits=waits, inc="WD1")
        wd1_after[d] = P.count["WD1"]

        waits = [("WD1", wd1_after[d])]
        if d >= 1:
            waits.append(("WDEC", wdec_after[d - 1]))
        P.add("scalar", lambda e: e.copy(out=dec_vm[:], in_=dec_b[:].bitcast(FP8)),
              waits=waits, inc="WD2")
        wd2 = P.count["WD2"]

        def dec2(d=d):
            def f(e):
                return e.scalar_tensor_tensor(out=wbf[d % 2][:], in0=dec_tm[:],
                                              scalar=1.0, in1=dec_vm[:],
                                              op0=ALU.add, op1=ALU.mult)
            return f
        waits = [("WD2", wd2)]
        if d >= 2:
            waits.append(("WTP", wtp_after[d - 2]))
        P.add("vector", dec2(), waits=waits, inc="WDEC")
        wdec_after[d] = P.count["WDEC"]

    emit_wtp(ND - 1)

    P.emit()
    nc.all_engine_barrier()

    esA.close()
    esBC = ExitStack()
    qkT = sbuf("qkT", [128, 2 * NP * S], BF16, esBC, side="right")
    vpad = [sbuf(f"vp{t}", [128, H * 65], BF16, esBC, side="right") for t in range(NT)]
    esB = ExitStack()
    pQK = [psum(f"pQK{i}", [128, 512], F32, esB) for i in range(2)]
    pV = [psum(f"pV{i}", [128, 1024], F32, esB) for i in range(2)]

    def qT_ap(p):
        return qkT[:, 2 * S * p: 2 * S * p + S]

    def kT_ap(p):
        return qkT[:, 2 * S * p + S: 2 * S * (p + 1)]

    for tt in range(NT):
        P.add("vector",
              (lambda tt: lambda e: e.memset(
                  vpad[tt][:].rearrange("p (h c) -> p h c", c=65)[:, :, 64:65], 1.0))(tt),
              inc="VONE")
    vone = P.count["VONE"]

    qkev_after = {}
    for j in range(16):
        for s4 in range(4):
            idx = 4 * j + s4
            bank = idx % 2

            def qkmm(j=j, s4=s4, bank=bank):
                def f(e):
                    inst = None
                    for d in range(ND):
                        inst = e.matmul(
                            out=pQK[bank][:],
                            lhsT=wqb[:, 3 * D * d + 128 * j: 3 * D * d + 128 * (j + 1)],
                            rhs=xT[:, S * d + 512 * s4: S * d + 512 * (s4 + 1)],
                            start=(d == 0), stop=(d == ND - 1))
                    return inst
                return f
            waits = []
            if idx >= 2:
                waits.append(("QKEV", qkev_after[idx - 2]))
            P.add("tensor", qkmm(), waits=waits, inc="QKMM")

            def qkev(j=j, s4=s4, bank=bank):
                def f(e):
                    p, isk = j % 8, j >= 8
                    dst = (kT_ap(p) if isk else qT_ap(p))[:, 512 * s4:512 * (s4 + 1)]
                    return e.copy(out=dst, in_=pQK[bank][:])
                return f
            P.add("scalar", qkev(), waits=[("QKMM", P.count["QKMM"])], inc="QKEV")
            qkev_after[idx] = P.count["QKEV"]

    vev_after = {}
    for tt in range(NT):
        bank = tt % 2

        def vmm(tt=tt, bank=bank):
            def f(e):
                inst = None
                for s2 in range(2):
                    for d in range(ND):
                        inst = e.matmul(
                            out=pV[bank][:, 512 * s2:512 * (s2 + 1)],
                            lhsT=xT[:, S * d + 128 * tt: S * d + 128 * (tt + 1)],
                            rhs=wqb[:, 3 * D * d + 2 * D + 512 * s2:
                                    3 * D * d + 2 * D + 512 * (s2 + 1)],
                            start=(d == 0), stop=(d == ND - 1))
                return inst
            return f
        waits = []
        if tt >= 2:
            waits.append(("VEV", vev_after[tt - 2]))
        P.add("tensor", vmm(), waits=waits, inc="VMM")

        def vev(tt=tt, bank=bank):
            def f(e):
                dst = vpad[tt][:].rearrange("p (h c) -> p h c", c=65)[:, :, 0:64]
                return e.tensor_copy(out=dst, in_=pV[bank][:].rearrange(
                    "p (h c) -> p h c", c=64))
            return f
        P.add("vector", vev(), waits=[("VMM", P.count["VMM"]), ("VONE", vone)],
              inc="VEV")
        vev_after[tt] = P.count["VEV"]

    if DBG:
        P.add("sync", lambda e: e.dma_start(out=dbg_qk[:],
                                            in_=qkT[:].bitcast(U8)),
              waits=[("QKEV", qkev_after[63]), ("VEV", vev_after[0])],
              inc="LD", inc_n=16)
        P.add("sync", lambda e: e.dma_start(out=dbg_v[:],
                                            in_=vpad[0][:].bitcast(U8)), inc="LD", inc_n=16)
    P.emit()
    nc.all_engine_barrier()

    esAB.close()
    esB.close()
    esC = ExitStack()
    attn = sbuf("attn", [128, NP * S], BF16)
    woutb = sbuf("woutb", [128, ND * D], BF16)
    PT = [sbuf(f"PT{i}", [128, 768], BF16, esC, side="right") for i in range(5)]
    sc65 = [sbuf(f"sc65_{h}", [65, S], F32, esC, side="right") for h in range(2)]
    wo_stage = [sbuf(f"wos{i}", [128, 512], F32, esC, side="right") for i in range(2)]
    scB = [sbuf(f"scB{i}", [128, 512], F32, esC, side="right") for i in range(2)]
    rscr = [sbuf(f"rscr{i}", [128, 512], F32, esC) for i in range(2)]
    pSb = [psum(f"pSb{i}", [128, 1024], F32, esC) for i in range(2)]
    pS = [pSb[0][:, 0:512], pSb[0][:, 512:1024], pSb[1][:, 0:512], pSb[1][:, 512:1024]]
    pPV = [psum(f"pPV{i}", [128, 512], F32, esC) for i in range(4)]
    pR2 = pPV

    woc_after = {}
    for dh in range(2 * ND):
        d, hf = dh // 2, dh % 2
        r = dh % 2
        waits = []
        if dh >= 2:
            waits.append(("WOC", woc_after[dh - 2]))
        P.add("sync",
              (lambda d, hf, r: lambda e: e.dma_start(
                  out=wo_stage[r][:],
                  in_=wout_d[128 * d:128 * (d + 1), 512 * hf:512 * (hf + 1)]))(d, hf, r),
              waits=waits, inc=f"WO{r}", inc_n=16)
        ldd = (f"WO{r}", P.count[f"WO{r}"])
        P.add("gpsimd",
              (lambda d, hf, r: lambda e: e.tensor_copy(
                  out=woutb[:, D * d + 512 * hf:D * d + 512 * (hf + 1)],
                  in_=wo_stage[r][:]))(d, hf, r),
              waits=[ldd], inc="WOC")
        woc_after[dh] = P.count["WOC"]

    mms_all = {}
    exp_after = {}
    msk_after = {}
    pvq_after = {}
    pve_after = {}
    nrm_after = {}
    r2_after = {}

    for p in range(NP):
        def emit_mmS(p, c):
            n_c = 128 * min(3, NC - c)
            bankA, bankB = pS[2 * (c % 2)], pS[2 * (c % 2) + 1]

            def f(e):
                e.matmul(out=bankA[0:128, 0:n_c],
                         lhsT=kT_ap(p)[0:64, 128 * c:128 * (c + 1)],
                         rhs=qT_ap(p)[0:64, 128 * c:128 * c + n_c],
                         start=True, stop=True, tile_position=(0, 0))
                return e.matmul(out=bankB[0:128, 0:n_c],
                                lhsT=kT_ap(p)[64:128, 128 * c:128 * (c + 1)],
                                rhs=qT_ap(p)[64:128, 128 * c:128 * c + n_c],
                                start=True, stop=True, tile_position=(64, 0))
            waits = []
            pp, cc = (p, c - 2) if c >= 2 else (p - 1, c + 14)
            if pp >= 0:
                waits.append(("EXP", 16 * pp + cc + 1))
            P.add("tensor", f, waits=waits, inc="MMS")
            return P.count["MMS"]

        def emit_exp(p, c, mms_count):
            n_c = 128 * min(3, NC - c)
            slot = PT[c % 5]
            bankA, bankB = pS[2 * (c % 2)], pS[2 * (c % 2) + 1]
            waits = [("MMS", mms_count)]
            if c >= 5:
                pt_reader = (p, c - 3)
            elif p > 0:
                prev_writer = c + 5 * ((15 - c) // 5)
                pt_reader = (p - 1, min(prev_writer + 2, 15))
            else:
                pt_reader = None
            if pt_reader in pvq_after:
                waits.append(("PVQ", pvq_after[pt_reader]))
            if DBG and p == DBG_P and c < 4:
                waits.append(("DBGD", 32))
            if DBG and ((p == DBG_P and c >= 4) or (p == DBG_P + 1)):
                waits.append(("DBGD", 32 + 16 * min(4, c if p == DBG_P else c + 16)))

            def f(e):
                src = pSb[c % 2][:].rearrange("p (b c2) -> p b c2", c2=512)[:, :, 0:n_c]
                dst = slot[:].rearrange("p (b c2) -> p b c2", c2=384)[:, :, 0:n_c]
                return e.activation(out=dst, in_=src, func=AF.Exp, scale=0.125)
            P.add("scalar", f, waits=waits, inc="EXP")
            exp_after[(p, c)] = P.count["EXP"]

        def emit_mask(p, c):
            slot = PT[c % 5]

            def f(e):
                pv = slot[:].rearrange("p (a c) -> p a c", c=128)
                md = maskD[:].rearrange("p (a c) -> p a c", c=128)
                inst = e.tensor_mul(out=pv[:, 0::3, :], in0=pv[:, 0::3, :], in1=md)
                if c <= 13:
                    mu = maskU[:].rearrange("p (a c) -> p a c", c=128)
                    inst = e.tensor_mul(out=pv[:, 2::3, :], in0=pv[:, 2::3, :], in1=mu)
                return inst
            P.add("vector", f, waits=[("EXP", exp_after[(p, c)])], inc="MSK")
            msk_after[(p, c)] = P.count["MSK"]
            if DBG and p == DBG_P and c < 4:
                P.add("sync",
                      (lambda c: lambda e: e.dma_start(
                          out=dbg_pt[c], in_=PT[c][:].bitcast(U8)))(c),
                      waits=[("MSK", msk_after[(p, c)])], inc="DBGD", inc_n=16)

        def emit_pv(p, t):
            g = t // 4
            waits = [("MSK", msk_after[(p, t)])]
            pg = (p, g - 2) if g >= 2 else ((p - 1, g + 2) if p > 0 else None)
            if pg is not None:
                for h in range(2):
                    if (pg[0], h, pg[1]) in pve_after:
                        waits.append(("PVE", pve_after[(pg[0], h, pg[1])]))
            if p > 0 and g <= 1:
                for gp in (2 * 0 + g % 2, 2 * 1 + g % 2):
                    if (p - 1, gp) in nrm_after:
                        waits.append(("NRM", nrm_after[(p - 1, gp)]))

            def f(e):
                inst = None
                for h in range(2):
                    bank = pPV[2 * h + g % 2]
                    ccs = range(max(0, t - 2), t + 1)
                    for i, cc in enumerate(ccs):
                        inst = e.matmul(
                            out=bank[0:65, 128 * (t % 4):128 * (t % 4 + 1)],
                            lhsT=vpad[cc][:, 65 * (2 * p + h):65 * (2 * p + h + 1)],
                            rhs=PT[cc % 5][0:128, 384 * h + 128 * (t - cc):
                                           384 * h + 128 * (t - cc) + 128],
                            start=(i == 0), stop=(cc == t))
                return inst
            P.add("tensor", f, waits=waits, inc="PVQ")
            pvq_after[(p, t)] = P.count["PVQ"]

        def emit_pve(p, h, g):
            waits = [("PVQ", pvq_after[(p, 4 * g + 3)])]
            if p > 0:
                waits.append(("NRM", nrm_after[(p - 1, g)]))

            def f(e, h=h, g=g):
                return e.tensor_copy(out=sc65[h][:, 512 * g:512 * (g + 1)],
                                     in_=pPV[2 * h + g % 2][0:65, 0:512])
            P.add("vector", f, waits=waits, inc="PVE")
            pve_after[(p, h, g)] = P.count["PVE"]

        def emit_normblock(p, g):
            nb = P.count.get("NRM", 0)
            sbw = [("PVE", pve_after[(p, 1, g)])]
            if DBG and p == DBG_P and g == 0:
                sbw.append(("DBGD", 48 + 16))
            prev = (p, g - 2) if g >= 2 else (p - 1, g + 2)
            if prev in nrm_after:
                sbw.append(("NRM", nrm_after[prev]))
            P.add("sync",
                  (lambda p, g, scB=scB, sc65=sc65: lambda e: e.dma_start(
                      out=scB[g % 2][64:128, :],
                      in_=sc65[1][0:64, 512 * g:512 * (g + 1)]))(p, g),
                  waits=sbw, inc=f"SB{g % 2}", inc_n=16)
            sbc = (f"SB{g % 2}", P.count[f"SB{g % 2}"])
            bankfree = pve_after[(p, g // 2, (g % 2) + 2)]
            zready = pve_after[(p, 1, g)]

            def bc(p=p, g=g, sc65=sc65):
                def f(e):
                    e.matmul(out=pR2[g][0:64, 0:512], lhsT=ones65f[64:65, 0:64],
                             rhs=sc65[0][64:65, 512 * g:512 * (g + 1)],
                             start=True, stop=True, tile_position=(64, 0))
                    return e.matmul(out=pR2[g][64:128, 0:512], lhsT=ones65f[64:65, 0:64],
                                    rhs=sc65[1][64:65, 512 * g:512 * (g + 1)],
                                    start=True, stop=True, tile_position=(64, 64))
                return f
            P.add("tensor", bc(), waits=[("PVE", max(bankfree, zready))], inc="R2")
            r2c = P.count["R2"]

            def rcp(g=g, nb=nb):
                def f(e):
                    e.activation(out=rscr[nb % 2][:], in_=pR2[g][:], func=AF.Ln)
                    return e.activation(out=rscr[nb % 2][:], in_=rscr[nb % 2][:],
                                        func=AF.Exp, scale=-1.0)
                return f
            P.add("scalar", rcp(),
                  waits=[("R2", r2c), ("NRM", P.count.get("NRM", 0) - 1)], inc="RB")
            rbc = P.count["RB"]

            def nrm(p=p, g=g, sc65=sc65, scB=scB):
                def f(e):
                    e.tensor_mul(out=attn[0:64, S * p + 512 * g:S * p + 512 * (g + 1)],
                                 in0=sc65[0][0:64, 512 * g:512 * (g + 1)],
                                 in1=rscr[nb % 2][0:64, 0:512])
                    return e.tensor_mul(
                        out=attn[64:128, S * p + 512 * g:S * p + 512 * (g + 1)],
                        in0=scB[g % 2][64:128, 0:512],
                        in1=rscr[nb % 2][64:128, 0:512])
                return f
            P.add("vector", nrm(), waits=[("RB", rbc), sbc], inc="NRM")
            nrm_after[(p, g)] = P.count["NRM"]

        mms_c = {}
        for c in range(3):
            mms_c[c] = emit_mmS(p, c)
        if DBG and p == DBG_P:
            def snap_ps(e):
                e.copy(out=wo_stage[0][:, 0:384], in_=pS[0][0:128, 0:384])
                return e.copy(out=wo_stage[1][:, 0:384], in_=pS[1][0:128, 0:384])
            P.add("scalar", snap_ps, waits=[("MMS", mms_c[0]), ("WOC", 8)], inc="DBGPS")
            P.add("sync", lambda e: e.dma_start(out=dbg_ps[0], in_=wo_stage[0][:, 0:512]),
                  waits=[("DBGPS", 1)], inc="DBGD", inc_n=16)
            P.add("sync", lambda e: e.dma_start(out=dbg_ps[1], in_=wo_stage[1][:, 0:512]),
                  inc="DBGD", inc_n=16)
        for c in range(NC):
            emit_exp(p, c, mms_c[c])
            emit_mask(p, c)
            emit_pv(p, c)
            if (c + 1) % 4 == 0:
                g = c // 4
                for h in range(2):
                    emit_pve(p, h, g)
                if g == 2:
                    emit_normblock(p, 0)
                    emit_normblock(p, 2)
                elif g == 3:
                    emit_normblock(p, 1)
                    emit_normblock(p, 3)
            if c + 3 < NC:
                mms_c[c + 3] = emit_mmS(p, c + 3)

        if DBG and p == DBG_P:
            def snap_pv(e):
                return e.copy(out=scB[0][0:65, 0:128], in_=pPV[0][0:65, 0:128])
            P.add("scalar", snap_pv, waits=[("PVQ", pvq_after[(p, 0)])], inc="DBGPV")
            P.add("sync", lambda e: e.dma_start(out=dbg_pv[:], in_=scB[0][0:65, 0:128]),
                  waits=[("DBGPV", 1)], inc="DBGD", inc_n=16)
        r2_after[p] = P.count["R2"]

    if DBG2:
        P.add("sync", lambda e: e.dma_start(out=dbg_at2[:], in_=attn[:].bitcast(U8)),
              waits=[("NRM", 32)], inc="LD", inc_n=16)
    if DBG:
        dbg_qk2 = nc.dram_tensor("dbg_qk2", [128, 4 * NP * S], U8, kind="ExternalOutput").ap()
        P.add("sync", lambda e: e.dma_start(out=dbg_qk2[:], in_=qkT[:].bitcast(U8)),
              waits=[("NRM", P.count["NRM"])], inc="LD", inc_n=16)
        P.add("sync", lambda e: e.dma_start(out=dbg_sc[0:65, :], in_=sc65[0][:]),
              waits=[("NRM", 4)], inc="LD", inc_n=16)
        P.add("sync", lambda e: e.dma_start(out=dbg_sc[65:130, :], in_=sc65[1][:]),
              inc="LD", inc_n=16)
        P.add("sync", lambda e: e.dma_start(out=dbg_at[:], in_=attn[:].bitcast(U8)),
              waits=[("NRM", P.count["NRM"])], inc="LD", inc_n=16)
        P.add("sync", lambda e: e.dma_start(out=dbg_wt[:], in_=WT[:].bitcast(U8)),
              inc="LD", inc_n=16)
    P.emit()
    nc.all_engine_barrier()

    esC.close()
    esBC.close()
    esD = ExitStack()
    x_res = [sbuf(f"xres{i}", [128, D], F32, esD, side="right") for i in range(4)]
    y1 = [sbuf(f"y1_{i}", [128, D], F32, esD, side="right") for i in range(3)]
    r1buf = [sbuf(f"r1_{i}", [128, D], F32, esD, side="right") for i in range(2)]
    r2buf = [sbuf(f"r2_{i}", [128, D], F32, esD, side="right") for i in range(2)]
    sq_scr = sbuf("sq_scr", [128, D], F32, esD, side="right")
    qin = [sbuf(f"qin{i}", [128, D], U8, esD, side="right") for i in range(2)]
    qinT = [sbuf(f"qinT{i}", [128, D], BF16, esD, side="right") for i in range(2)]
    outb = [sbuf(f"outb{i}", [128, D], F32, esD, side="right") for i in range(4)]
    stat = sbuf("stat", [128, 16], F32, esD, side="right")
    eps_c = sbuf("eps_c", [128, 1], F32, esD, side="right")
    pPR = [psum("pPR0", [128, 1024], F32, esD)] * 2
    pFF_ = [psum(f"pFF{i}", [128, 1024], F32, esD) for i in range(2)]
    pQ8 = [psum(f"pQ8{i}", [128, 1024], FP8, esD) for i in range(2)]

    P.add("vector", lambda e: e.memset(eps_c[:], 1e-5), inc="EPS")

    DF = float(D)
    EPS = 1e-5
    ST = {"s1": 0, "sq1": 1, "negm1": 2, "eb1": 3, "rstd1": 4, "xswsC": 5,
          "amax": 6, "chalf": 7, "xsws": 8, "s2": 9, "sq2": 10, "negm2": 11,
          "eb2": 12, "rstd2": 13, "xswsB": 15}
    XSW = ["xsws", "xswsB", "xswsC"]
    bnbuf = sbuf("bnbuf", [128, 12], F32, esD, side="right")

    def stc(name):
        return stat[:, ST[name]:ST[name] + 1]

    ldx_after = {}
    sq1_after = {}
    sq2_after = {}
    prmm_after = {}
    r1_after = {}
    ln1s_after = {}
    ln1r_after = {}
    y1_after = {}
    q8_after = {}
    qtp_after = {}
    qte_after = {}
    ffmm_after = {}
    ff1_after = {}
    r2s_after = {}
    ln2s_after = {}
    ln2r_after = {}
    as2_after = {}
    outv_after = {}
    out_after = {}
    st_count = 0

    def emit_xr(tt):
        xslot = tt % 4
        waits = []
        if tt >= 4:
            waits.append(("R1", r1_after.get(tt - 4, 0)))
        P.add("sync",
              (lambda tt, xslot: lambda e: e.dma_start(
                  out=x_res[xslot][:], in_=x_d[128 * tt:128 * (tt + 1), :]))(tt, xslot),
              waits=waits, inc=f"XR{xslot}", inc_n=16)
        ldx_after[tt] = (f"XR{xslot}", P.count[f"XR{xslot}"])

    def emit_preA(tt):
        bank = tt % 2
        def prmm(tt=tt, bank=bank):
            def f(e):
                inst = None
                for s2 in range(2):
                    for d in range(ND):
                        inst = e.matmul(
                            out=pPR[bank][:, 512 * s2:512 * (s2 + 1)],
                            lhsT=attn[:, S * d + 128 * tt:S * d + 128 * (tt + 1)],
                            rhs=woutb[:, D * d + 512 * s2:D * d + 512 * (s2 + 1)],
                            start=(d == 0), stop=(d == ND - 1))
                return inst
            return f
        waits = [("WOC", 16)]
        if tt >= 1:
            waits.append(("R1", r1_after[tt - 1]))
        P.add("tensor", prmm(), waits=waits, inc="PRMM")
        prmm_after[tt] = P.count["PRMM"]

        def r1op(tt=tt, bank=bank):
            def f(e):
                return e.scalar_tensor_tensor(
                    out=r1buf[bank][:], in0=pPR[bank][:], scalar=1.0,
                    in1=x_res[tt % 4][:], op0=ALU.mult, op1=ALU.add)
            return f
        r1w = [("PRMM", prmm_after[tt]), ldx_after[tt], ("EPS", 1)]
        if tt >= 2:
            r1w.append(("Y1", y1_after.get(tt - 2, 0)))
        P.add("vector", r1op(), waits=r1w, inc="R1")
        r1_after[tt] = P.count["R1"]

        def ln1stats(bank=bank):
            def f(e):
                e.bn_stats(out=bnbuf[:, 0:6], in_=r1buf[bank][:, 0:512])
                fence(e, "vector", e.bn_stats(out=bnbuf[:, 6:12], in_=r1buf[bank][:, 512:1024]))
                fence(e, "vector", e.bn_aggr(out=stat[:, 0:2], in_=bnbuf[:]))
                return e.tensor_scalar(out=stc("negm1"), in0=stc("s1"), scalar1=-1.0,
                                       scalar2=None, op0=ALU.mult)
            return f
        P.add("vector", ln1stats(), waits=[("R1", r1_after[tt])], inc="LN1S")
        ln1s_after[tt] = P.count["LN1S"]

        def ln1rstd(e):
            e.activation(out=stc("rstd1"), in_=stc("sq1"), func=AF.Ln, bias=eps_c[:])
            return e.activation(out=stc("rstd1"), in_=stc("rstd1"), func=AF.Exp, scale=-0.5)
        lrw = [("LN1S", ln1s_after[tt])]
        if tt >= 1:
            lrw.append(("Y1", y1_after.get(tt - 1, 0)))
        P.add("scalar", ln1rstd, waits=lrw, inc="LN1R")
        ln1r_after[tt] = P.count["LN1R"]

        def y1op(tt=tt, bank=bank):
            def f(e):
                return e.tensor_scalar(out=y1[tt % 3][:], in0=r1buf[bank][:],
                                       scalar1=stc("negm1"), scalar2=stc("rstd1"),
                                       op0=ALU.add, op1=ALU.mult)
            return f
        waits = [("LN1R", ln1r_after[tt])]
        if tt >= 3:
            waits.append(("FF1", ff1_after.get(tt - 3, 0)))
        P.add("vector", y1op(), waits=waits, inc="Y1")
        y1_after[tt] = P.count["Y1"]

        if DBG2:
            P.add("sync", (lambda tt, bank: lambda e: e.dma_start(
                out=dbg_r1[128 * tt:128 * (tt + 1), :], in_=r1buf[bank][:]))(tt, bank),
                  waits=[("Y1", y1_after[tt])], inc="LD", inc_n=16)
        def quant(tt=tt):
            def f(e):
                fence(e, "vector", e.tensor_reduce(out=stc("amax"), in_=y1[tt % 3][:],
                                                   axis=mybir.AxisListType.X, op=ALU.max,
                                                   apply_absolute_value=True))
                fence(e, "vector", e.reciprocal(out=stc("chalf"), in_=stc("amax")))
                e.tensor_scalar(out=stc("chalf"), in0=stc("chalf"), scalar1=224.0,
                                scalar2=None, op0=ALU.mult)
                return e.scalar_tensor_tensor(
                    out=stc(XSW[tt % 3]), in0=stc("amax"),
                    scalar=1.0 / 448.0, in1=ws_col[:], op0=ALU.mult, op1=ALU.mult)
            return f
        waits = [("Y1", y1_after[tt]), ("BIASF", 2)]
        P.add("vector", quant(), waits=waits, inc="CHF")
        chf = P.count["CHF"]

        def qmul(tt=tt):
            def f(e):
                return e.mul(out=qin[tt % 2][:].bitcast(FP8), in_=y1[tt % 3][:],
                             mul=stc("chalf"))
            return f
        qw = [("CHF", chf)]
        if tt >= 2:
            qw.append(("QTP", qtp_after.get(tt - 2, 0)))
        P.add("scalar", qmul(), waits=qw, inc="Q8")
        q8_after[tt] = P.count["Q8"]

    def emit_preB(tt):
        bank = tt % 2
        for half in range(2):
            def qtp(tt=tt, half=half):
                def f(e):
                    inst = None
                    for dd in range(4):
                        d = 4 * half + dd
                        inst = e.transpose(
                            out=pQ8[half][0:128, 256 * dd:256 * (dd + 1):2],
                            in_=qin[tt % 2][:, 128 * d:128 * (d + 1)].bitcast(FP8),
                            identity=id8[:])
                    return inst
                return f
            waits = [("Q8", q8_after[tt])]
            if tt >= 1:
                waits.append(("QTE", qte_after.get((tt - 1, half), 0)))
            P.add("tensor", qtp(), waits=waits, inc="QTP")
            qtp_after[tt] = P.count["QTP"]

            def qte(tt=tt, half=half):
                def f(e):
                    dst = qinT[tt % 2][:].rearrange("p (d t) -> p d t", t=128)[
                        :, 4 * half:4 * half + 4, :]
                    src = pQ8[half][:].rearrange("p (d t) -> p d t", t=256)[:, :, 0:256:2]
                    return e.activation(out=dst, in_=src, func=AF.Copy, scale=2.0)
                return f
            qtew = [("QTP", P.count["QTP"])]
            if tt >= 2:
                qtew.append(("FFMM", ffmm_after.get(tt - 2, 0)))
            P.add("scalar", qte(), waits=qtew, inc="QTE")
            qte_after[(tt, half)] = P.count["QTE"]

        if DBG2:
            P.add("sync", (lambda tt: lambda e: e.dma_start(
                out=dbg_y1b[128 * tt:128 * (tt + 1), :], in_=y1[tt % 3][:]))(tt),
                  waits=[("Q8", q8_after[tt])], inc="LD", inc_n=16)
            P.add("sync", (lambda tt: lambda e: e.dma_start(
                out=dbg_stt[tt], in_=stat[:]))(tt),
                  waits=[("Q8", q8_after[tt])], inc="LD", inc_n=16)
    def emit_post1(tt):
        def ffmm(tt=tt):
            def f(e):
                inst = None
                for s2 in range(2):
                    for d in range(ND):
                        inst = e.matmul(
                            out=pFF_[tt % 2][:, 512 * s2:512 * (s2 + 1)],
                            lhsT=qinT[tt % 2][:, 128 * d:128 * (d + 1)],
                            rhs=WT[:, D * d + 512 * s2:D * d + 512 * (s2 + 1)],
                            start=(d == 0), stop=(d == ND - 1))
                return inst
            return f
        waits = [("QTE", qte_after[(tt, 1)])]
        if tt >= 2:
            waits.append(("FF1", ff1_after.get(tt - 2, 0)))
        P.add("tensor", ffmm(), waits=waits, inc="FFMM")
        ffmm_after[tt] = P.count["FFMM"]

        def ff1(tt=tt):
            def f(e):
                return e.scalar_tensor_tensor(
                    out=r2buf[tt % 2][:], in0=pFF_[tt % 2][:],
                    scalar=stc(XSW[tt % 3]),
                    in1=y1[tt % 3][:], op0=ALU.mult, op1=ALU.add)
            return f
        ffw = [("FFMM", ffmm_after[tt])]
        if tt >= 2:
            ffw.append(("AS2", as2_after.get(tt - 2, 0)))
        P.add("vector", ff1(), waits=ffw, inc="FF1")
        ff1_after[tt] = P.count["FF1"]

    def emit_post2(tt):
        bank = tt % 2
        def r2op(tt=tt):
            def f(e):
                return e.tensor_add(out=r2buf[tt % 2][:], in0=r2buf[tt % 2][:],
                                    in1=bias_full[:])
            return f
        P.add("gpsimd", r2op(), waits=[("FF1", ff1_after[tt]), ("BIASF", 1)], inc="R2S")
        r2s_after[tt] = P.count["R2S"]

        if DBG2:
            P.add("sync", (lambda tt: lambda e: e.dma_start(
                out=dbg_t2[128 * tt:128 * (tt + 1), :], in_=r2buf[tt % 2][:]))(tt),
                  waits=[("R2S", P.count["R2S"])], inc="LD", inc_n=16)

        def as2(tt=tt):
            def f(e):
                e.activation(out=sq_scr[:], in_=r2buf[tt % 2][:], func=AF.Square,
                             accum_out=stc("sq2"))
                return e.activation(out=sq_scr[:], in_=r2buf[tt % 2][:], func=AF.Copy,
                                    accum_out=stc("s2"))
            return f
        asw = [("R2S", r2s_after[tt])]
        if tt >= 1:
            asw.append(("LN2S", ln2s_after.get(tt - 1, 0)))
        P.add("scalar", as2(), waits=asw, inc="AS2")
        as2c = P.count["AS2"]
        as2_after[tt] = as2c

        if DBG and tt == 0:
            P.add("sync", (lambda tt: lambda e: e.dma_start(
                out=dbg_y1[:], in_=y1[tt % 3][:]))(tt),
                  waits=[("Y1", y1_after[tt])], inc="LD", inc_n=16)

    def emit_post3(tt):
        def ln2smalls(e):
            e.tensor_mul(out=stc("eb2"), in0=stc("s2"), in1=stc("s2"))
            e.tensor_scalar(out=stc("eb2"), in0=stc("eb2"),
                            scalar1=-1.0 / (DF * DF), scalar2=EPS,
                            op0=ALU.mult, op1=ALU.add)
            return e.tensor_scalar(out=stc("negm2"), in0=stc("s2"),
                                   scalar1=-1.0 / DF, scalar2=None, op0=ALU.mult)
        P.add("vector", ln2smalls, waits=[("AS2", as2_after[tt])], inc="LN2S")
        ln2s_after[tt] = P.count["LN2S"]

        def ln2rstd(e):
            e.activation(out=stc("rstd2"), in_=stc("sq2"), func=AF.Ln,
                         scale=1.0 / DF, bias=stc("eb2"))
            return e.activation(out=stc("rstd2"), in_=stc("rstd2"), func=AF.Exp, scale=-0.5)
        l2w = [("LN2S", ln2s_after[tt])]
        if tt >= 1:
            l2w.append(("OUTV", outv_after.get(tt - 1, 0)))
        P.add("scalar", ln2rstd, waits=l2w, inc="LN2R")
        ln2r_after[tt] = P.count["LN2R"]

        def outop(tt=tt):
            def f(e):
                return e.tensor_scalar(out=outb[tt % 4][:], in0=r2buf[tt % 2][:],
                                       scalar1=stc("negm2"), scalar2=stc("rstd2"),
                                       op0=ALU.add, op1=ALU.mult)
            return f
        waits = [("LN2R", ln2r_after[tt])]
        if tt >= 4:
            waits.append(out_after[tt - 4])
        P.add("vector", outop(), waits=waits, inc="OUTV")
        outv = P.count["OUTV"]
        outv_after[tt] = outv
        P.add("sync",
              (lambda tt: lambda e: e.dma_start(
                  out=out_d[128 * tt:128 * (tt + 1), :], in_=outb[tt % 4][:]))(tt),
              waits=[("OUTV", outv)], inc=f"OD{tt % 2}", inc_n=16)
        out_after[tt] = (f"OD{tt % 2}", P.count[f"OD{tt % 2}"])


    for tt in range(NT + 4):
        if tt < NT:
            emit_xr(tt)
        if tt >= 4:
            emit_post3(tt - 4)
        if 3 <= tt <= NT + 2:
            emit_post2(tt - 3)
        if tt < NT:
            emit_preA(tt)
        if 1 <= tt <= NT:
            emit_preB(tt - 1)
        if 2 <= tt <= NT + 1:
            emit_post1(tt - 2)

    P.add("vector", lambda e: e.tensor_copy(out=sq_scr[:].bitcast(mybir.dt.int32)[0:32, 0:1],
                                            in_=stat[:].bitcast(mybir.dt.int32)[0:32, 0:1]),
          waits=[("LN2S", ln2s_after[NT - 1])])

    final_waits = [(sn, c) for sn, c in P.count.items()
                   if sn.startswith(("XD", "WQ", "W8", "WO", "XR", "OD", "SB")) or
                   sn in ("CONST", "LD", "DBGD")]
    P.add("sync", lambda e: None, waits=final_waits)
    P.emit()
    return nc


_NC_CACHE = None


def kernel(L0_attn_x, L0_attn_w_qkv, L0_attn_w_out, L0_ffn_weight,
           L0_ffn_weight_scale, L0_ffn_bias):
    global _NC_CACHE
    if _NC_CACHE is None:
        _NC_CACHE = build_nc()
    nc = _NC_CACHE

    x = np.ascontiguousarray(np.asarray(L0_attn_x))
    wqkv = np.ascontiguousarray(np.asarray(L0_attn_w_qkv))
    wout = np.ascontiguousarray(np.asarray(L0_attn_w_out))
    wffn = np.ascontiguousarray(np.asarray(L0_ffn_weight)).view(np.uint8)
    bias = np.ascontiguousarray(
        np.broadcast_to(np.asarray(L0_ffn_bias).reshape(1, D), (128, D)))
    wsc = np.ascontiguousarray(
        np.broadcast_to(np.asarray(L0_ffn_weight_scale).reshape(1, 1), (128, 1)))

    in_maps = []
    for i in range(8):
        in_maps.append({
            "x": np.ascontiguousarray(x[i]),
            "wqkv": wqkv, "wout": wout, "wffn": wffn,
            "bias": bias, "wscale": wsc,
        })
    res = run_bass_kernel_spmd(nc, in_maps, core_ids=list(range(8)))
    return np.stack([r["out"] for r in res.results], axis=0)


